# version 21
# baseline (speedup 1.0000x reference)
import math
import numpy as np
import concourse.bass as bass
import concourse.mybir as mybir
from concourse.bass_utils import run_bass_kernel_spmd

F32 = mybir.dt.float32
I32 = mybir.dt.int32
BF16 = mybir.dt.bfloat16
AOP = mybir.AluOpType
ACT = mybir.ActivationFunctionType
AX = mybir.AxisListType

NCORES = 8
T = 2048
D = 1024
TCH = T // 128
ECH = D // 128
RANK = 16
TWO_PI = float(2.0 * np.pi)
KB = 1024

ENGINES = ["sync", "gpsimd", "vector", "scalar", "tensor"]


class _DummyInst:
    def then_inc(self, *a, **k):
        return self


class _Dummy:
    def __getattr__(self, name):
        return lambda *a, **k: _DummyInst()


class Sched:

    def __init__(self, sems):
        self.sems = sems
        self.counts = {k: 0 for k in sems}
        self.events = {}
        self.mode = "count"

    def inc(self, inst, sem, n=1, ev=None):
        self.counts[sem] += n
        if ev is not None:
            if self.mode == "count":
                assert ev not in self.events, f"dup event {ev}"
                self.events[ev] = (sem, self.counts[sem])
            else:
                assert self.events[ev] == (sem, self.counts[sem]), (ev, self.events[ev], sem, self.counts[sem])
        if self.mode == "emit" and inst is not None and not isinstance(inst, _DummyInst):
            inst.then_inc(self.sems[sem], n)
        return inst

    def wait(self, g, ev):
        if self.mode == "emit":
            assert ev in self.events, f"unknown event {ev}"
            sem, val = self.events[ev]
            g.wait_ge(self.sems[sem], val)

    def last(self, sem):
        return ("__last__", sem, self.counts[sem])

    def wait_last(self, g, sem_val):
        if self.mode == "emit":
            _, sem, val = sem_val
            g.wait_ge(self.sems[sem], val)

    def start_emit(self):
        self.counts = {k: 0 for k in self.counts}
        self.mode = "emit"


class NS:
    pass


def build_program(n_seq=4, debug_scratch=False):
    nc = bass.Bass("TRN2", target_bir_lowering=False)
    t = NS()
    t.n_seq = n_seq

    t.x = nc.dram_tensor("x", [n_seq * T, D], F32, kind="ExternalInput").ap()
    t.y = nc.dram_tensor("y", [n_seq * T, D], F32, kind="ExternalOutput").ap()
    t.wqT = nc.dram_tensor("wqT", [D, D], F32, kind="ExternalInput").ap()
    t.wkT = nc.dram_tensor("wkT", [D, D], F32, kind="ExternalInput").ap()
    t.wvT = nc.dram_tensor("wvT", [D, D], F32, kind="ExternalInput").ap()
    t.kern = nc.dram_tensor("kern", [D, D], F32, kind="ExternalInput").ap()
    t.a_in = nc.dram_tensor("a_in", [D, RANK], F32, kind="ExternalInput").ap()
    t.b_in = nc.dram_tensor("b_in", [RANK, D], F32, kind="ExternalInput").ap()
    t.bias_in = nc.dram_tensor("bias_in", [1, D], F32, kind="ExternalInput").ap()
    t.bv_in = nc.dram_tensor("bv_in", [1, D], F32, kind="ExternalInput").ap()
    t.bq_l = nc.dram_tensor("bq_l", [128, ECH], F32, kind="ExternalInput").ap()
    t.bk_l = nc.dram_tensor("bk_l", [128, ECH], F32, kind="ExternalInput").ap()
    t.pb_l = nc.dram_tensor("pb_l", [128, ECH], F32, kind="ExternalInput").ap()
    t.uscale_in = nc.dram_tensor("uscale_in", [128, ECH], F32, kind="ExternalInput").ap()

    skind = "ExternalOutput" if debug_scratch else "Internal"
    for nm in ["s_cq_hi", "s_cq_lo", "s_sq_hi", "s_sq_lo",
               "s_ck_hi", "s_ck_lo", "s_sk_hi", "s_sk_lo"]:
        setattr(t, nm, nc.dram_tensor(nm, [n_seq, ECH, 128, T], BF16, kind=skind).ap())
    t.s_v_hi = nc.dram_tensor("s_v_hi", [n_seq, TCH, 128, D], BF16, kind=skind).ap()
    t.s_v_lo = nc.dram_tensor("s_v_lo", [n_seq, TCH, 128, D], BF16, kind=skind).ap()
    t.s_acc = nc.dram_tensor("s_acc", [T, D], F32, kind=skind).ap()
    t.debug_scratch = debug_scratch
    if debug_scratch:
        t.s_kh1 = nc.dram_tensor("s_kh1", [T, D], F32, kind="ExternalOutput").ap()
        t.s_l = nc.dram_tensor("s_l", [128, 32], F32, kind="ExternalOutput").ap()
        t.s_part = nc.dram_tensor("s_part", [n_seq * T, D], F32, kind="ExternalOutput").ap()
        t.s_xn = nc.dram_tensor("s_xn", [n_seq * 128, D], F32, kind="ExternalOutput").ap()
        t.s_st = nc.dram_tensor("s_st", [n_seq * 128, 4], F32, kind="ExternalOutput").ap()
    for nm in ["s_wq_hi", "s_wq_lo", "s_wk_hi", "s_wk_lo",
               "s_wv_hi", "s_wv_lo", "s_kq_hi", "s_kq_lo"]:
        setattr(t, nm, nc.dram_tensor(nm, [ECH, 128, D], BF16, kind=skind).ap())

    ARENA = 212736
    arena = nc.alloc_sbuf_tensor("arena", [128, ARENA], mybir.dt.uint8)
    arena_addr = nc.lookup_mloc(arena).addr

    def place(name, shape, dtype, off):
        per_part = int(np.prod(shape[1:])) * mybir.dt.size(dtype)
        assert off + per_part <= ARENA, (name, off, per_part)
        h = nc.virtual_sbuf_tensor(name, list(shape), dtype)
        nc._tensor(h.name, list(shape), dtype, type="SB")
        mloc = nc.lookup_mloc(h)
        mloc.allocated = True
        mloc.addr = arena_addr + off
        return h

    o = [0]

    def pers(name, shape, dtype):
        per_part = int(np.prod(shape[1:])) * mybir.dt.size(dtype)
        h = place(name, shape, dtype, o[0])
        o[0] += (per_part + 31) // 32 * 32
        return h

    t.ident_b = pers("ident_b", [128, 128], BF16)
    t.ident_f = pers("ident_f", [128, 128], F32)
    t.ones_row_b = pers("ones_row_b", [1, 128], BF16)
    t.ones_row_f = pers("ones_row_f", [1, 128], F32)
    t.ones_col_b = pers("ones_col_b", [128, 1], BF16)
    t.c2pi = pers("c2pi", [128, 1], F32)
    t.cinv32 = pers("cinv32", [128, 1], F32)
    t.ceps = pers("ceps", [128, 1], F32)
    t.uscale = pers("uscale", [128, ECH], F32)
    t.bq_sb = pers("bq_sb", [128, ECH], F32)
    t.bk_sb = pers("bk_sb", [128, ECH], F32)
    t.pb_sb = pers("pb_sb", [128, ECH], F32)
    t.ub_qs = pers("ub_qs", [128, ECH], F32)
    t.ub_qc = pers("ub_qc", [128, ECH], F32)
    t.ub_ks = pers("ub_ks", [128, ECH], F32)
    t.ub_kc = pers("ub_kc", [128, ECH], F32)
    t.pb2 = pers("pb2", [128, ECH], F32)
    t.a_f = pers("a_f", [128, ECH * RANK], F32)
    t.a_hi = pers("a_hi", [128, ECH * RANK], BF16)
    t.a_lo = pers("a_lo", [128, ECH * RANK], BF16)
    t.b_hi = pers("b_hi", [RANK, D], BF16)
    t.b_lo = pers("b_lo", [RANK, D], BF16)
    t.bias_hi = pers("bias_hi", [1, D], BF16)
    t.bias_lo = pers("bias_lo", [1, D], BF16)
    t.bv_hi = pers("bv_hi", [1, D], BF16)
    t.bv_lo = pers("bv_lo", [1, D], BF16)
    t.t1_hi = pers("t1_hi", [RANK, T], BF16)
    t.t1_lo = pers("t1_lo", [RANK, T], BF16)
    for nm in ["ssum", "sqs", "mcol", "vcol", "sdcol", "rstd", "am", "msq",
               "gmax", "scale_col", "iscale_col", "ltc", "rcc"]:
        setattr(t, nm, pers(nm, [128, 1], F32))
    t.mxr = pers("mxr", [1, 128], F32)
    t.mx1 = pers("mx1", [1, 1], F32)
    t.vjunk = pers("vjunk", [128, 1], F32)
    t.qth = pers("qth", [128, 14], F32)
    t.sjunk = pers("sjunk", [128, 1], F32)
    t.dbgb = pers("dbgb", [128, D], F32)
    t.dbgl = pers("dbgl", [128, 32], F32)
    t.lsb = pers("lsb", [128, 16], F32)
    assert o[0] <= 29 * KB + 768, o[0]

    AL = (o[0] + 1023) // 1024 * 1024
    assert AL + 178 * KB <= ARENA, AL

    t.xnT_hi = place("xnT_hi", [128, ECH * T], BF16, AL + 0)
    t.xnT_lo = place("xnT_lo", [128, ECH * T], BF16, AL + 32 * KB)
    t.w_hi = place("w_hi", [128, ECH * D], BF16, AL + 64 * KB)
    t.w_lo = place("w_lo", [128, ECH * D], BF16, AL + 80 * KB)
    t.xb = [place("xb0", [128, D], F32, AL + 96 * KB),
            place("xb1", [128, D], F32, AL + 100 * KB)]
    t.po = [place("po0", [128, D], F32, AL + 104 * KB),
            place("po1", [128, D], F32, AL + 108 * KB)]
    t.u_s = place("u_s", [128, T], F32, AL + 112 * KB)
    t.u_c = place("u_c", [128, T], F32, AL + 120 * KB)
    t.n_i = place("n_i", [128, T], I32, AL + 128 * KB)
    t.sin_f = place("sin_f", [128, T], F32, AL + 136 * KB)
    t.n_i2 = place("n_i2", [128, T], I32, AL + 96 * KB)
    t.sin_f2 = place("sin_f2", [128, T], F32, AL + 104 * KB)
    t.tg = [place(f"tg{i}", [128, T], BF16, AL + 144 * KB + i * 4 * KB) for i in range(4)]
    t.v_f32 = place("v_f32", [128, D], F32, AL + 160 * KB)
    t.v_h = place("v_h", [128, D], BF16, AL + 164 * KB)
    t.v_l = place("v_l", [128, D], BF16, AL + 166 * KB)
    t.b_f = place("b_f", [RANK, D], F32, AL + 160 * KB)
    t.bias_f = place("bias_f", [1, D], F32, AL + 164 * KB)
    t.bv_f = place("bv_f", [1, D], F32, AL + 168 * KB)
    t.xsq = place("xsq", [128, D], F32, AL + 112 * KB)
    t.xn_f = place("xn_f", [128, D], F32, AL + 120 * KB)
    t.wstage = place("wstage", [128, ECH * D], F32, AL + 0)
    t.wstageB = place("wstageB", [128, ECH * D], F32, AL + 32 * KB)
    t.qr1 = place("qr1", [128, ECH * D], F32, AL + 64 * KB)
    t.qr2 = place("qr2", [128, ECH * D], F32, AL + 96 * KB)
    t.wc_hi = place("wc_hi", [128, ECH * D], BF16, AL + 128 * KB)
    t.wc_lo = place("wc_lo", [128, ECH * D], BF16, AL + 144 * KB)
    H = T // 2
    t.cq_hi = place("cq_hi", [128, ECH * 512], BF16, AL + 0)
    t.cq_lo = place("cq_lo", [128, ECH * 512], BF16, AL + 8 * KB)
    t.sq_hi = place("sq_hi", [128, ECH * 512], BF16, AL + 16 * KB)
    t.sq_lo = place("sq_lo", [128, ECH * 512], BF16, AL + 24 * KB)
    t.pt_hi = place("pt_hi", [128, 8 * 512], BF16, AL + 32 * KB)
    t.pt_lo = place("pt_lo", [128, 8 * 512], BF16, AL + 40 * KB)
    t.ob = [place("ob0", [128, D], F32, AL + 48 * KB),
            place("ob1", [128, D], F32, AL + 52 * KB)]
    t.pab = [place("pab0", [128, D], F32, AL + 56 * KB),
             place("pab1", [128, D], F32, AL + 60 * KB)]
    t.vb_hi = place("vb_hi", [128, 8 * D], BF16, AL + 64 * KB)
    t.vb_lo = place("vb_lo", [128, 8 * D], BF16, AL + 80 * KB)
    t.sk_lo = place("sk_lo", [128, ECH * H], BF16, AL + 96 * KB)
    t.ck_hi = place("ck_hi", [128, ECH * H], BF16, AL + 112 * KB)
    t.ck_lo = place("ck_lo", [128, ECH * H], BF16, AL + 128 * KB)
    t.sk_hi = place("sk_hi", [128, ECH * H], BF16, AL + 144 * KB)
    t.p_f = place("p_f", [128, 512], F32, AL + 160 * KB)
    t.accb = [place("accb0", [128, D], F32, AL + 162 * KB),
              place("accb1", [128, D], F32, AL + 166 * KB)]
    t.xpre = [place("xpre0", [128, D], F32, AL + 170 * KB),
              place("xpre1", [128, D], F32, AL + 174 * KB)]

    t.pA = nc.alloc_psum_tensor("pA", [128, 2048], F32)
    t.pB = nc.alloc_psum_tensor("pB", [128, 2048], F32)

    sems = {nm: nc.alloc_semaphore(nm) for nm in ["LD", "ST", "TE", "VE", "SE", "GP"]}
    P = Sched(sems)
    for e in ENGINES:
        _program(t, P, e, _Dummy())
    P.start_emit()
    with nc.Block() as block:
        @block.sync
        def _(g):
            _program(t, P, "sync", g)

        @block.gpsimd
        def _(g):
            _program(t, P, "gpsimd", g)

        @block.vector
        def _(g):
            _program(t, P, "vector", g)

        @block.scalar
        def _(g):
            _program(t, P, "scalar", g)

        @block.tensor
        def _(g):
            _program(t, P, "tensor", g)
    return nc, t


def _mm_group(g, specs):
    seen_first = {}
    lasts = {}
    for i, (out, lhs, rhs, key) in enumerate(specs):
        lasts[key] = i
    inst = None
    for i, (out, lhs, rhs, key) in enumerate(specs):
        st = key not in seen_first
        seen_first[key] = True
        inst = g.matmul(out, lhs, rhs, start=st, stop=(lasts[key] == i),
                        skip_group_check=True)
    return inst




def _emit_po(t, P, g_v, g_s, s, ft, on):
    row0 = s * T + ft * 128
    xbv = t.xpre[ft] if ft < 2 else t.xb[ft % 2]
    pov = t.po[ft % 2]
    if on("vector"):
        P.wait(g_v, ("ffn", s, ft))
        if ft >= 2:
            P.wait(g_v, ("post", s, ft - 2))
        pF = (t.pA if ft % 2 == 0 else t.pB)[:, 1024:2048]
        P.inc(g_v.tensor_tensor(pov[:], xbv[:], pF, AOP.add), "VE", 1, ev=("po", s, ft))
    if on("scalar"):
        P.wait(g_s, ("po", s, ft))
        if t.debug_scratch:
            P.inc(g_s.dma_start(out=t.s_part[row0:row0 + 128, :], in_=pov[:]), "ST", 16)
        P.inc(g_s.dma_start(out=t.y[row0:row0 + 128, :], in_=pov[:]),
              "ST", 16, ev=("post", s, ft))


def _emit_ffn_main(t, P, g, s, ft):
    P.wait(g, ("kqload", s))
    if ft >= 2:
        P.wait(g, ("po", s, ft - 2))
    pF = (t.pA if ft % 2 == 0 else t.pB)[:, 1024:2048]
    specs = []
    for dch in range(ECH):
        for xx, ww in [(t.xnT_hi, t.w_hi), (t.xnT_hi, t.w_lo), (t.xnT_lo, t.w_hi)]:
            for nh in range(2):
                specs.append((pF[:, nh * 512:(nh + 1) * 512],
                              xx[:, dch * T + ft * 128: dch * T + (ft + 1) * 128],
                              ww[:, dch * D + nh * 512: dch * D + (nh + 1) * 512],
                              nh))
    seen = {}
    lastinst = None
    for out, lhs, rhs, key in specs:
        st = key not in seen
        seen[key] = True
        lastinst = g.matmul(out, lhs, rhs, start=st, stop=False, skip_group_check=True)
    P.inc(lastinst, "TE", 1, ev=("ffnm", s, ft))


def _emit_ffn_lora(t, P, g, s, ft):
    P.wait(g, ("t1c", s, ft))
    pF = (t.pA if ft % 2 == 0 else t.pB)[:, 1024:2048]
    specs = []
    for t1x, bb in [(t.t1_hi, t.b_hi), (t.t1_hi, t.b_lo), (t.t1_lo, t.b_hi)]:
        for nh in range(2):
            specs.append((pF[:, nh * 512:(nh + 1) * 512],
                          t1x[:, ft * 128:(ft + 1) * 128],
                          bb[:, nh * 512:(nh + 1) * 512], nh))
    for bb in [t.bias_hi, t.bias_lo]:
        for nh in range(2):
            specs.append((pF[:, nh * 512:(nh + 1) * 512],
                          t.ones_row_b[:], bb[:, nh * 512:(nh + 1) * 512], nh))
    lasts = {}
    for i, (out, lhs, rhs, key) in enumerate(specs):
        lasts[key] = i
    lastinst = None
    for i, (out, lhs, rhs, key) in enumerate(specs):
        lastinst = g.matmul(out, lhs, rhs, start=False, stop=(lasts[key] == i),
                            skip_group_check=True)
    P.inc(lastinst, "TE", 1, ev=("ffn", s, ft))


def _program(t, P, eng, g):
    on = lambda n: eng == n
    n_seq = t.n_seq
    H = T // 2

    if on("gpsimd"):
        g.memset(t.ident_b[:], 0.0)
        g.affine_select(out=t.ident_b[:], in_=t.ident_b[:], compare_op=AOP.not_equal,
                        fill=1.0, base=0, pattern=[[-1, 128]], channel_multiplier=1)
        g.memset(t.ident_f[:], 0.0)
        g.affine_select(out=t.ident_f[:], in_=t.ident_f[:], compare_op=AOP.not_equal,
                        fill=1.0, base=0, pattern=[[-1, 128]], channel_multiplier=1)
        g.memset(t.ones_row_b[:], 1.0)
        g.memset(t.ones_row_f[:], 1.0)
        g.memset(t.ones_col_b[:], 1.0)
        g.memset(t.c2pi[:], TWO_PI)
        g.memset(t.cinv32[:], 1.0 / 32.0)
        P.inc(g.memset(t.ceps[:], 1e-5), "GP", 1, ev="consts")

    if on("sync"):
        for ap_in, ap_sb in [(t.bq_l, t.bq_sb), (t.bk_l, t.bk_sb), (t.pb_l, t.pb_sb),
                             (t.uscale_in, t.uscale), (t.b_in, t.b_f),
                             (t.bias_in, t.bias_f), (t.bv_in, t.bv_f)]:
            P.inc(g.dma_start(out=ap_sb[:], in_=ap_in[:]), "LD", 16)
        P.inc(g.dma_start(out=t.a_f[:].rearrange("p (c r) -> p c r", c=ECH),
                          in_=t.a_in.rearrange("(c p) r -> p c r", p=128)),
              "LD", 16, ev="pl_small")

    if on("vector"):
        P.wait(g, "pl_small")
        P.wait(g, "consts")
        g.tensor_single_scalar(t.pb2[:], t.pb_sb[:], 1.0 / TWO_PI, AOP.mult)
        g.tensor_tensor(t.ub_qs[:], t.bq_sb[:], t.uscale[:], AOP.mult)
        g.tensor_tensor(t.ub_ks[:], t.bk_sb[:], t.uscale[:], AOP.mult)
        g.tensor_tensor(t.ub_qs[:], t.ub_qs[:], t.pb2[:], AOP.add)
        g.tensor_tensor(t.ub_ks[:], t.ub_ks[:], t.pb2[:], AOP.add)
        g.tensor_single_scalar(t.ub_qc[:], t.ub_qs[:], 0.25, AOP.add)
        g.tensor_single_scalar(t.ub_kc[:], t.ub_ks[:], 0.25, AOP.add)
        for f, hi, lo in [(t.a_f, t.a_hi, t.a_lo), (t.b_f, t.b_hi, t.b_lo),
                          (t.bias_f, t.bias_hi, t.bias_lo), (t.bv_f, t.bv_hi, t.bv_lo)]:
            g.tensor_copy(hi[:], f[:])
            P.inc(g.tensor_tensor(lo[:], f[:], hi[:], AOP.subtract), "VE", 1)
        P.inc(g.nop(), "VE", 1, ev="pl_vec")

    wspecs = [("kq", t.kern, t.s_kq_hi, t.s_kq_lo, True),
              ("wq", t.wqT, t.s_wq_hi, t.s_wq_lo, False),
              ("wk", t.wkT, t.s_wk_hi, t.s_wk_lo, False),
              ("wv", t.wvT, t.s_wv_hi, t.s_wv_lo, False)]
    for wi, (nm, src, dst_hi, dst_lo, quant) in enumerate(wspecs):
        wst = t.wstage if wi % 2 == 0 else t.wstageB
        if on("sync"):
            if wi > 1:
                P.wait(g, f"wcast_{wspecs[wi-2][0]}")
            P.inc(g.dma_start(out=wst[:].rearrange("p (c e) -> p c e", c=ECH),
                              in_=src.rearrange("(c p) e -> p c e", p=128)),
                  "LD", 16, ev=f"wload_{nm}")
        if quant:
            if on("vector"):
                P.wait(g, f"wload_{nm}")
                g.tensor_reduce(t.gmax[:], t.wstage[:], axis=AX.X, op=AOP.max,
                                apply_absolute_value=True)
                P.inc(g.tensor_copy(t.vjunk[:], t.c2pi[:]), "VE", 1, ev="q_mx")
            if on("tensor"):
                P.wait(g, "q_mx")
                P.wait(g, "consts")
                P.inc(g.matmul(t.pA[0:1, 0:128], t.gmax[:], t.ident_f[:],
                               start=True, stop=True), "TE", 1, ev="q_tr")
            if on("vector"):
                P.wait(g, "q_tr")
                g.tensor_copy(t.mxr[:], t.pA[0:1, 0:128])
                g.tensor_copy(t.vjunk[:], t.c2pi[:])
                g.tensor_reduce(t.mx1[:], t.mxr[:], axis=AX.X, op=AOP.max,
                                apply_absolute_value=True)
                P.inc(g.tensor_copy(t.vjunk[:], t.c2pi[:]), "VE", 1, ev="q_mx1")
            if on("tensor"):
                P.wait(g, "q_mx1")
                P.inc(g.matmul(t.pA[:, 512:513], t.ones_row_f[:], t.mx1[:],
                               start=True, stop=True), "TE", 1, ev="q_bc")
            if on("vector"):
                P.wait(g, "q_bc")
                g.tensor_copy(t.gmax[:], t.pA[:, 512:513])
                g.tensor_copy(t.vjunk[:], t.c2pi[:])
                g.tensor_single_scalar(t.scale_col[:], t.gmax[:],
                                       float(np.float32(1.0) / np.float32(7.0)), AOP.mult)
                g.tensor_copy(t.vjunk[:], t.c2pi[:])
                for k in range(7):
                    g.tensor_single_scalar(t.qth[:, k:k + 1], t.scale_col[:],
                                           k + 0.5, AOP.mult)
                for k in range(7):
                    g.tensor_single_scalar(t.qth[:, 7 + k:8 + k], t.scale_col[:],
                                           -(k + 0.5), AOP.mult)
                g.tensor_copy(t.vjunk[:], t.c2pi[:])
                g.tensor_single_scalar(t.qr1[:], t.wstage[:], t.qth[:, 0:1], AOP.is_ge)
                g.tensor_single_scalar(t.qr2[:], t.wstage[:], t.qth[:, 7:8], AOP.is_le)
                for k in range(1, 7):
                    g.scalar_tensor_tensor(t.qr1[:], t.wstage[:], t.qth[:, k:k + 1],
                                           t.qr1[:], AOP.is_ge, AOP.add)
                    g.scalar_tensor_tensor(t.qr2[:], t.wstage[:], t.qth[:, 7 + k:8 + k],
                                           t.qr2[:], AOP.is_le, AOP.add)
                g.tensor_tensor(t.qr1[:], t.qr1[:], t.qr2[:], AOP.subtract)
                g.tensor_copy(t.vjunk[:], t.c2pi[:])
                P.inc(g.tensor_single_scalar(t.wstage[:], t.qr1[:], t.scale_col[:],
                                             AOP.mult), "VE", 1, ev="q_done")
        if on("gpsimd"):
            if quant:
                P.wait(g, "q_done")
            else:
                P.wait(g, f"wload_{nm}")
            if wi > 0:
                P.wait(g, f"wspill_{wspecs[wi-1][0]}")
            g.tensor_copy(t.wc_hi[:], wst[:])
            P.inc(g.tensor_tensor(t.wc_lo[:], wst[:], t.wc_hi[:],
                                  AOP.subtract), "GP", 1, ev=f"wcast_{nm}")
        if on("scalar"):
            P.wait(g, f"wcast_{nm}")
            P.inc(g.dma_start(out=dst_hi.rearrange("c p e -> p c e"),
                              in_=t.wc_hi[:].rearrange("p (c e) -> p c e", c=ECH)),
                  "ST", 16)
            P.inc(g.dma_start(out=dst_lo.rearrange("c p e -> p c e"),
                              in_=t.wc_lo[:].rearrange("p (c e) -> p c e", c=ECH)),
                  "ST", 16, ev=f"wspill_{nm}")

    for s in range(n_seq):
        if on("sync"):
            P.wait(g, "wspill_kq")
            if s > 0:
                P.wait(g, ("B_done", s - 1))
            P.inc(g.dma_start(out=t.w_hi[:].rearrange("p (c e) -> p c e", c=ECH),
                              in_=t.s_kq_hi.rearrange("c p e -> p c e")), "LD", 16)
            P.inc(g.dma_start(out=t.w_lo[:].rearrange("p (c e) -> p c e", c=ECH),
                              in_=t.s_kq_lo.rearrange("c p e -> p c e")),
                  "LD", 16, ev=("kqload", s))

        for tt in range(TCH):
            row0 = s * T + tt * 128
            xb = t.xpre[tt] if tt < 2 else t.xb[tt % 2]
            po = t.po[tt % 2]
            if on("sync"):
                if tt < 2:
                    if s == 0:
                        if tt == 0:
                            P.wait(g, "wspill_wv")
                        P.inc(g.dma_start(out=xb[:], in_=t.x[row0:row0 + 128, :]),
                              "LD", 16, ev=("xload", s, tt))
                else:
                    if s == 0 and tt == 2:
                        P.wait(g, "q_done")
                    if tt >= 4:
                        P.wait(g, ("po", s, tt - 2))
                    P.inc(g.dma_start(out=xb[:], in_=t.x[row0:row0 + 128, :]),
                          "LD", 16, ev=("xload", s, tt))
            if on("vector"):
                if s == 0 and tt == 0:
                    P.wait(g, "pl_vec")
                if s > 0 and tt == 0:
                    P.wait(g, ("outst", s - 1, 3, 3))
                P.wait(g, ("xload", s, tt))
                g.tensor_tensor(t.xsq[:], xb[:], xb[:], AOP.mult)
                g.tensor_reduce(t.ssum[:], xb[:], axis=AX.X, op=AOP.add)
                g.tensor_reduce(t.sqs[:], t.xsq[:], axis=AX.X, op=AOP.add)
                g.tensor_single_scalar(t.mcol[:], t.ssum[:], 1.0 / D, AOP.mult)
                g.tensor_single_scalar(t.vcol[:], t.sqs[:], 1.0 / D, AOP.mult)
                g.tensor_tensor(t.msq[:], t.mcol[:], t.mcol[:], AOP.mult)
                g.tensor_copy(t.vjunk[:], t.c2pi[:])
                g.tensor_tensor(t.vcol[:], t.vcol[:], t.msq[:], AOP.subtract)
                g.tensor_copy(t.vjunk[:], t.c2pi[:])
                g.tensor_copy(t.vjunk[:], t.c2pi[:])
                P.inc(g.tensor_copy(t.vjunk[:], t.c2pi[:]), "VE", 1, ev=("var", s, tt))
            if on("scalar"):
                if t.debug_scratch and tt == 1:
                    P.wait(g, ("xn", s, 0))
                    P.inc(g.dma_start(out=t.s_xn[s * 128:(s + 1) * 128, :], in_=t.xn_f[:]),
                          "ST", 16)
                    P.inc(g.dma_start(out=t.s_st[s * 128:(s + 1) * 128, :], in_=t.dbgl[:, 0:4]),
                          "ST", 16, ev=("dbgxn", s))
                P.wait(g, ("var", s, tt))
                if tt == 0:
                    g.activation(t.sjunk[:], t.c2pi[:], ACT.Sqrt)
                    g.activation(t.sjunk[:], t.c2pi[:], ACT.Sqrt)
                g.activation(t.sdcol[:], t.vcol[:], ACT.Sqrt, bias=t.ceps[:])
                g.copy(t.sjunk[:], t.c2pi[:])
                P.inc(g.copy(t.sjunk[:], t.c2pi[:]), "SE", 1, ev=("sd", s, tt))
            if on("vector"):
                P.wait(g, ("sd", s, tt))
                if t.debug_scratch and tt == 1:
                    P.wait(g, ("dbgxn", s))
                if tt > 0:
                    P.wait(g, ("trl", s, tt - 1))
                g.reciprocal(t.rstd[:], t.sdcol[:])
                g.tensor_copy(t.vjunk[:], t.c2pi[:])
                g.tensor_tensor(t.am[:], t.mcol[:], t.rstd[:], AOP.mult)
                g.tensor_copy(t.vjunk[:], t.c2pi[:])
                g.tensor_copy(t.vjunk[:], t.c2pi[:])
                if t.debug_scratch and tt == 0:
                    g.tensor_copy(t.dbgl[:, 0:1], t.rstd[:])
                    g.tensor_copy(t.dbgl[:, 1:2], t.am[:])
                    g.tensor_copy(t.dbgl[:, 2:3], t.mcol[:])
                    g.tensor_copy(t.dbgl[:, 3:4], t.vcol[:])
                P.inc(g.tensor_scalar(t.xn_f[:], xb[:], t.rstd[:], t.am[:],
                                      AOP.mult, AOP.subtract), "VE", 1, ev=("xn", s, tt))
            if on("tensor"):
                if tt >= 1:
                    _emit_ffn_main(t, P, g, s, tt - 1)
                P.wait(g, ("xn", s, tt))
                if tt > 0:
                    P.wait(g, ("xnT", s, tt - 1))
                if s == 0 and tt == 0:
                    P.wait(g, "consts")
                for dch in range(ECH):
                    inst = g.transpose(t.pA[:, dch * 128:(dch + 1) * 128],
                                       t.xn_f[:, dch * 128:(dch + 1) * 128], t.ident_f[:])
                P.inc(inst, "TE", 1, ev=("trl", s, tt))
            if on("vector"):
                P.wait(g, ("trl", s, tt))
                if s == 0 and tt == 0:
                    P.wait(g, "wcast_wv")
                xnT_hi_v = t.xnT_hi[:].rearrange("p (c n) -> p c n", c=ECH)
                xnT_lo_v = t.xnT_lo[:].rearrange("p (c n) -> p c n", c=ECH)
                tr_v = t.pA[:, 0:1024].rearrange("p (c n) -> p c n", c=ECH)
                g.tensor_copy(xnT_hi_v[:, :, tt * 128:(tt + 1) * 128], tr_v)
                P.inc(g.tensor_tensor(xnT_lo_v[:, :, tt * 128:(tt + 1) * 128], tr_v,
                                      xnT_hi_v[:, :, tt * 128:(tt + 1) * 128],
                                      AOP.subtract),
                      "VE", 1, ev=("xnT", s, tt))
            if on("tensor"):
                P.wait(g, ("xnT", s, tt))
                if s == 0 and tt == 0:
                    P.wait(g, "pl_vec")
                if tt > 0:
                    P.wait(g, ("t1c", s, tt - 1))
                t1ps = t.pB[0:RANK, 0:128]
                specs = []
                for dch in range(ECH):
                    for xx, aa in [(t.xnT_hi, t.a_hi), (t.xnT_hi, t.a_lo), (t.xnT_lo, t.a_hi)]:
                        specs.append((t1ps,
                                      aa[:, dch * RANK:(dch + 1) * RANK],
                                      xx[:, dch * T + tt * 128: dch * T + (tt + 1) * 128],
                                      0))
                P.inc(_mm_group(g, specs), "TE", 1, ev=("t1ps", s, tt))
                if tt >= 1:
                    _emit_ffn_lora(t, P, g, s, tt - 1)
            if on("vector"):
                P.wait(g, ("t1ps", s, tt))
                t1ps = t.pB[0:RANK, 0:128]
                g.tensor_copy(t.t1_hi[:, tt * 128:(tt + 1) * 128], t1ps)
                P.inc(g.tensor_tensor(t.t1_lo[:, tt * 128:(tt + 1) * 128], t1ps,
                                      t.t1_hi[:, tt * 128:(tt + 1) * 128], AOP.subtract),
                      "VE", 1, ev=("t1c", s, tt))
            if tt >= 1:
                _emit_po(t, P, g, g, s, tt - 1, on)


        if on("tensor"):
            _emit_ffn_main(t, P, g, s, TCH - 1)
            _emit_ffn_lora(t, P, g, s, TCH - 1)
        _emit_po(t, P, g, g, s, TCH - 1, on)

        for which, srcs, spill_sin, spill_cos, ub_s, ub_c in [
                ("q", (t.s_wq_hi, t.s_wq_lo), (t.s_sq_hi, t.s_sq_lo),
                 (t.s_cq_hi, t.s_cq_lo), t.ub_qs, t.ub_qc),
                ("k", (t.s_wk_hi, t.s_wk_lo), (t.s_sk_hi, t.s_sk_lo),
                 (t.s_ck_hi, t.s_ck_lo), t.ub_ks, t.ub_kc)]:
            if on("sync"):
                P.wait(g, f"wspill_w{which}")
                if which == "q":
                    P.wait(g, ("ffn", s, TCH - 1))
                else:
                    P.wait(g, ("qk", s, "q", ECH - 1))
                P.inc(g.dma_start(out=t.w_hi[:].rearrange("p (c e) -> p c e", c=ECH),
                                  in_=srcs[0].rearrange("c p e -> p c e")), "LD", 16)
                P.inc(g.dma_start(out=t.w_lo[:].rearrange("p (c e) -> p c e", c=ECH),
                                  in_=srcs[1].rearrange("c p e -> p c e")),
                      "LD", 16, ev=("wld", s, which))
            for ec in range(ECH):
                pp = t.pA if ec % 2 == 0 else t.pB
                if on("tensor"):
                    P.wait(g, ("wld", s, which))
                    if which == "q" and ec < 2:
                        P.wait(g, ("po", s, TCH - 1))
                    prev = None
                    if ec >= 2:
                        prev = (s, which, ec - 2)
                    elif which == "k":
                        prev = (s, "q", ec + ECH - 2)
                    if prev is not None:
                        P.wait(g, ("uev", *prev))
                    specs = []
                    for dch in range(ECH):
                        for xx, ww in [(t.xnT_hi, t.w_hi), (t.xnT_hi, t.w_lo), (t.xnT_lo, t.w_hi)]:
                            for nq in range(4):
                                specs.append((pp[:, nq * 512:(nq + 1) * 512],
                                              ww[:, dch * D + ec * 128: dch * D + (ec + 1) * 128],
                                              xx[:, dch * T + nq * 512: dch * T + (nq + 1) * 512],
                                              nq))
                    P.inc(_mm_group(g, specs), "TE", 1, ev=("qk", s, which, ec))
                if on("vector"):
                    P.wait(g, ("qk", s, which, ec))
                    if ec > 0:
                        P.wait(g, ("spill", s, which, ec - 1))
                    elif which == "k":
                        P.wait(g, ("spill", s, "q", ECH - 1))
                    g.tensor_scalar(t.u_s[:], pp[:], t.uscale[:, ec:ec + 1],
                                    ub_s[:, ec:ec + 1], AOP.mult, AOP.add)
                    P.inc(g.tensor_scalar(t.u_c[:], pp[:], t.uscale[:, ec:ec + 1],
                                          ub_c[:, ec:ec + 1], AOP.mult, AOP.add),
                          "VE", 1, ev=("uev", s, which, ec))
                if on("gpsimd"):
                    P.wait(g, ("uev", s, which, ec))
                    if which == "q" and ec == 0:
                        P.wait(g, ("post", s, TCH - 1))
                    g.tensor_copy(t.n_i[:], t.u_s[:])
                    P.inc(g.tensor_tensor(t.sin_f[:], t.u_s[:], t.n_i[:], AOP.subtract),
                          "GP", 1, ev=("fs", s, which, ec))
                    g.tensor_copy(t.n_i2[:], t.u_c[:])
                    P.inc(g.tensor_tensor(t.sin_f2[:], t.u_c[:], t.n_i2[:], AOP.subtract),
                          "GP", 1, ev=("fc", s, which, ec))
                if on("scalar"):
                    P.wait(g, ("fs", s, which, ec))
                    if which == "q" and ec == 0:
                        g.activation(t.sjunk[:], t.c2pi[:], ACT.Sin)
                        g.activation(t.sjunk[:], t.c2pi[:], ACT.Sin)
                    P.inc(g.activation(t.u_s[:], t.sin_f[:], ACT.Sin, scale=t.c2pi[:]),
                          "SE", 1, ev=("sinact", s, which, ec))
                if on("vector"):
                    P.wait(g, ("sinact", s, which, ec))
                    g.tensor_copy(t.tg[0][:], t.u_s[:])
                    g.tensor_tensor(t.tg[1][:], t.u_s[:], t.tg[0][:], AOP.subtract)
                if on("scalar"):
                    P.wait(g, ("fc", s, which, ec))
                    P.inc(g.activation(t.u_c[:], t.sin_f2[:], ACT.Sin, scale=t.c2pi[:]),
                          "SE", 1, ev=("cosact", s, which, ec))
                if on("vector"):
                    P.wait(g, ("cosact", s, which, ec))
                    g.tensor_copy(t.tg[2][:], t.u_c[:])
                    P.inc(g.tensor_tensor(t.tg[3][:], t.u_c[:], t.tg[2][:], AOP.subtract),
                          "VE", 1, ev=("trig", s, which, ec))
                if on("scalar"):
                    P.wait(g, ("trig", s, which, ec))
                    P.inc(g.dma_start(out=spill_sin[0][s, ec], in_=t.tg[0][:]), "ST", 16)
                    P.inc(g.dma_start(out=spill_sin[1][s, ec], in_=t.tg[1][:]), "ST", 16)
                    P.inc(g.dma_start(out=spill_cos[0][s, ec], in_=t.tg[2][:]), "ST", 16)
                    P.inc(g.dma_start(out=spill_cos[1][s, ec], in_=t.tg[3][:]),
                          "ST", 16, ev=("spill", s, which, ec))

        if on("sync"):
            P.wait(g, "wspill_wv")
            P.wait(g, ("qk", s, "k", ECH - 1))
            P.inc(g.dma_start(out=t.w_hi[:].rearrange("p (c e) -> p c e", c=ECH),
                              in_=t.s_wv_hi.rearrange("c p e -> p c e")), "LD", 16)
            P.inc(g.dma_start(out=t.w_lo[:].rearrange("p (c e) -> p c e", c=ECH),
                              in_=t.s_wv_lo.rearrange("c p e -> p c e")),
                  "LD", 16, ev=("wld", s, "v"))
        if on("sync"):
            P.wait(g, ("spill", s, "k", ECH - 1))
            P.wait(g, ("trig", s, "k", ECH - 1))
            P.wait(g, ("cosact", s, "k", ECH - 1))
            for di, (dst, ssrc) in enumerate([(t.ck_hi, t.s_ck_hi), (t.ck_lo, t.s_ck_lo),
                                              (t.sk_hi, t.s_sk_hi), (t.sk_lo, t.s_sk_lo)]):
                P.inc(g.dma_start(out=dst[:].rearrange("p (c n) -> p c n", c=ECH),
                                  in_=ssrc[s][:, :, 0:H].rearrange("c p n -> p c n")),
                      "LD", 16, ev=("cks", s, 0) if di == 3 else None)
        for kc in range(TCH):
            pv = (t.pA if kc % 2 == 0 else t.pB)[:, 0:1024]
            if on("tensor"):
                P.wait(g, ("wld", s, "v"))
                if kc < 2:
                    P.wait(g, ("uev", s, "k", ECH - 2 + kc))
                else:
                    P.wait(g, ("vev", s, kc - 2))
                specs = []
                for dch in range(ECH):
                    for xx, ww in [(t.xnT_hi, t.w_hi), (t.xnT_hi, t.w_lo), (t.xnT_lo, t.w_hi)]:
                        for nh in range(2):
                            specs.append((pv[:, nh * 512:(nh + 1) * 512],
                                          xx[:, dch * T + kc * 128: dch * T + (kc + 1) * 128],
                                          ww[:, dch * D + nh * 512: dch * D + (nh + 1) * 512],
                                          nh))
                for bb in [t.bv_hi, t.bv_lo]:
                    for nh in range(2):
                        specs.append((pv[:, nh * 512:(nh + 1) * 512],
                                      t.ones_row_b[:], bb[:, nh * 512:(nh + 1) * 512], nh))
                P.inc(_mm_group(g, specs), "TE", 1, ev=("vps", s, kc))
            if on("vector"):
                P.wait(g, ("vps", s, kc))
                if kc == 0:
                    P.wait(g, ("spill", s, "k", ECH - 1))
                else:
                    P.wait(g, ("vspill", s, kc - 1))
                g.tensor_copy(t.v_f32[:], pv)
                g.tensor_copy(t.v_h[:], t.v_f32[:])
                P.inc(g.tensor_tensor(t.v_l[:], t.v_f32[:], t.v_h[:], AOP.subtract),
                      "VE", 1, ev=("vev", s, kc))
            if on("scalar"):
                P.wait(g, ("vev", s, kc))
                P.inc(g.dma_start(out=t.s_v_hi[s, kc], in_=t.v_h[:]), "ST", 16)
                P.inc(g.dma_start(out=t.s_v_lo[s, kc], in_=t.v_l[:]),
                      "ST", 16, ev=("vspill", s, kc))

        for kh in range(2):
            if on("sync"):
                if kh == 0:
                    P.wait(g, ("vps", s, TCH - 1))
                else:
                    P.wait(g, ("sc", s, 0, 3, 7))
                    for di, (dst, ssrc) in enumerate([(t.ck_hi, t.s_ck_hi), (t.ck_lo, t.s_ck_lo),
                                                      (t.sk_hi, t.s_sk_hi), (t.sk_lo, t.s_sk_lo)]):
                        P.inc(g.dma_start(out=dst[:].rearrange("p (c n) -> p c n", c=ECH),
                                          in_=ssrc[s][:, :, H:T].rearrange("c p n -> p c n")),
                              "LD", 16, ev=("cks", s, 1) if di == 3 else None)
                for di, (dst, ssrc) in enumerate([(t.cq_hi, t.s_cq_hi), (t.cq_lo, t.s_cq_lo),
                                                  (t.sq_hi, t.s_sq_hi), (t.sq_lo, t.s_sq_lo)]):
                    P.inc(g.dma_start(out=dst[:].rearrange("p (c n) -> p c n", c=ECH),
                                      in_=ssrc[s][:, :, 0:512].rearrange("c p n -> p c n")),
                          "LD", 16, ev=("cq", s, kh, 0) if di == 3 else None)
                if kh == 0:
                    P.wait(g, ("vspill", s, TCH - 1))
                    P.wait(g, ("vev", s, TCH - 1))
                else:
                    P.wait(g, ("attn", s, 0, 3, 3))
                for di, (dst, ssrc) in enumerate([(t.vb_hi, t.s_v_hi), (t.vb_lo, t.s_v_lo)]):
                    P.inc(g.dma_start(out=dst[:].rearrange("p (c n) -> p c n", c=8),
                                      in_=ssrc[s][kh * 8:(kh + 1) * 8].rearrange("c p n -> p c n")),
                          "LD", 16, ev=("ckv", s, kh) if di == 1 else None)
            for qm in range(4):
                if on("sync"):
                    if qm > 0:
                        P.wait(g, ("sc", s, kh, qm - 1, 7))
                        for di, (dst, ssrc) in enumerate([(t.cq_hi, t.s_cq_hi), (t.cq_lo, t.s_cq_lo),
                                                          (t.sq_hi, t.s_sq_hi), (t.sq_lo, t.s_sq_lo)]):
                            P.inc(g.dma_start(out=dst[:].rearrange("p (c n) -> p c n", c=ECH),
                                              in_=ssrc[s][:, :, qm * 512:(qm + 1) * 512].rearrange("c p n -> p c n")),
                                  "LD", 16, ev=("cq", s, kh, qm) if di == 3 else None)
                    if kh == 1:
                        for qc in range(4):
                            r0 = (qm * 4 + qc) * 128
                            P.wait(g, ("accst", s, 3, 3))
                            if not (qm == 0 and qc < 2):
                                pq = (qm, qc - 2) if qc >= 2 else (qm - 1, qc + 2)
                                P.wait(g, ("aev", s, 1, *pq))
                            P.inc(g.dma_start(out=t.accb[qc % 2][:],
                                              in_=t.s_acc[r0:r0 + 128, :]),
                                  "LD", 16, ev=("accld", s, qm, qc))
                            P.inc(g.dma_start(out=t.pab[qc % 2][:],
                                              in_=t.y[s * T + r0: s * T + r0 + 128, :]),
                                  "LD", 16, ev=("pald", s, qm, qc))
                for kc in range(8):
                    ps = t.pA[:, (kc % 2) * 512:(kc % 2) * 512 + 512]
                    if on("tensor"):
                        if qm == 0 and kc == 0:
                            P.wait(g, ("cks", s, kh))
                        P.wait(g, ("cq", s, kh, qm))
                        if kc >= 2:
                            P.wait(g, ("exp", s, kh, qm, kc - 2))
                        elif qm > 0 or kh > 0:
                            pqm = (s, kh, qm - 1) if qm > 0 else (s, kh - 1, 3)
                            P.wait(g, ("exp", *pqm, 6 + kc))
                        specs = []
                        for ec in range(ECH):
                            for ckx, cqx in [(t.ck_hi, t.cq_hi), (t.ck_lo, t.cq_hi),
                                             (t.ck_hi, t.cq_lo), (t.sk_hi, t.sq_hi),
                                             (t.sk_lo, t.sq_hi), (t.sk_hi, t.sq_lo)]:
                                specs.append((ps,
                                              ckx[:, ec * H + kc * 128: ec * H + (kc + 1) * 128],
                                              cqx[:, ec * 512:(ec + 1) * 512], 0))
                        P.inc(_mm_group(g, specs), "TE", 1, ev=("sc", s, kh, qm, kc))
                    if on("scalar"):
                        P.wait(g, ("sc", s, kh, qm, kc))
                        if kc >= 1:
                            P.wait(g, ("pt", s, kh, qm, kc - 1))
                        if kh == 0 and qm == 0 and kc == 0:
                            g.activation(t.sjunk[:], t.c2pi[:], ACT.Exp)
                            g.activation(t.sjunk[:], t.c2pi[:], ACT.Exp)
                        P.inc(g.activation(t.p_f[:], ps, ACT.Exp, scale=t.cinv32[:]),
                              "SE", 1, ev=("exp", s, kh, qm, kc))
                    if on("vector"):
                        P.wait(g, ("exp", s, kh, qm, kc))
                        g.tensor_copy(t.pt_hi[:, kc * 512:(kc + 1) * 512], t.p_f[:])
                        P.inc(g.tensor_tensor(t.pt_lo[:, kc * 512:(kc + 1) * 512], t.p_f[:],
                                              t.pt_hi[:, kc * 512:(kc + 1) * 512],
                                              AOP.subtract),
                              "VE", 1, ev=("pt", s, kh, qm, kc))
                for qc in range(4):
                    pa_ = t.pB[:, (qc % 2) * 1024:(qc % 2) * 1024 + 1024]
                    lcol = t.pA[:, 1024 + (qc % 2) * 512 + qc: 1024 + (qc % 2) * 512 + qc + 1]
                    if on("tensor"):
                        P.wait(g, ("pt", s, kh, qm, 7))
                        if qm == 0 and qc == 0:
                            P.wait(g, ("ckv", s, kh))
                        if qc >= 2:
                            P.wait(g, ("aev", s, kh, qm, qc - 2))
                        elif qm > 0 or kh > 0:
                            pqm = (s, kh, qm - 1) if qm > 0 else (s, kh - 1, 3)
                            P.wait(g, ("aev", *pqm, qc + 2))
                        if qm > 0 or kh > 0:
                            pqm = (s, kh, qm - 1) if qm > 0 else (s, kh - 1, 3)
                            P.wait(g, ("lread", *pqm, qc))
                        specs = []
                        for kc in range(8):
                            for px, vx, dol in [(t.pt_hi, t.vb_hi, True),
                                                (t.pt_hi, t.vb_lo, False),
                                                (t.pt_lo, t.vb_hi, True)]:
                                lhs = px[:, kc * 512 + qc * 128: kc * 512 + (qc + 1) * 128]
                                for nh in range(2):
                                    specs.append((pa_[:, nh * 512:(nh + 1) * 512], lhs,
                                                  vx[:, kc * D + nh * 512: kc * D + (nh + 1) * 512],
                                                  nh))
                                if dol:
                                    specs.append((lcol, lhs, t.ones_col_b[:], 2))
                        P.inc(_mm_group(g, specs), "TE", 1, ev=("attn", s, kh, qm, qc))
                    if on("vector"):
                        P.wait(g, ("attn", s, kh, qm, qc))
                        if kh == 0:
                            ab = t.accb[qc % 2]
                            if not (qm == 0 and qc < 2):
                                pq = (qm, qc - 2) if qc >= 2 else (qm - 1, qc + 2)
                                P.wait(g, ("accst", s, *pq))
                            g.tensor_copy(ab[:], pa_)
                            P.inc(g.tensor_copy(t.lsb[:, qm * 4 + qc: qm * 4 + qc + 1],
                                                lcol), "VE", 1, ev=("lread", s, kh, qm, qc))
                            P.inc(g.nop(), "VE", 1, ev=("aev", s, kh, qm, qc))
                        else:
                            ob = t.ob[qc % 2]
                            if t.debug_scratch and s == 0:
                                if not (qm == 0 and qc == 0):
                                    pq2 = (qm, qc - 1) if qc >= 1 else (qm - 1, 3)
                                    P.wait(g, ("dbgst", s, *pq2))
                                g.tensor_copy(t.dbgb[:], pa_)
                                g.tensor_copy(t.dbgl[:, qm * 8 + qc: qm * 8 + qc + 1], lcol)
                                g.tensor_copy(t.dbgl[:, qm * 8 + 4 + qc: qm * 8 + 4 + qc + 1],
                                              t.lsb[:, qm * 4 + qc: qm * 4 + qc + 1])
                            g.tensor_tensor(t.ltc[:], lcol,
                                            t.lsb[:, qm * 4 + qc: qm * 4 + qc + 1], AOP.add)
                            g.tensor_copy(t.vjunk[:], t.c2pi[:])
                            g.reciprocal(t.rcc[:], t.ltc[:])
                            P.inc(g.tensor_copy(t.vjunk[:], t.c2pi[:]), "VE", 1,
                                  ev=("lread", s, kh, qm, qc))
                            P.wait(g, ("accld", s, qm, qc))
                            P.wait(g, ("pald", s, qm, qc))
                            if not (qm == 0 and qc < 2):
                                pq = (qm, qc - 2) if qc >= 2 else (qm - 1, qc + 2)
                                P.wait(g, ("outst", s, *pq))
                            elif s > 0:
                                P.wait(g, ("outst", s - 1, 3, qc + 2))
                            g.tensor_tensor(ob[:], t.accb[qc % 2][:], pa_, AOP.add)
                            g.tensor_single_scalar(ob[:], ob[:], t.rcc[:], AOP.mult)
                            P.inc(g.tensor_tensor(ob[:], ob[:], t.pab[qc % 2][:], AOP.add),
                                  "VE", 1, ev=("aev", s, kh, qm, qc))
                    if on("scalar"):
                        P.wait(g, ("aev", s, kh, qm, qc))
                        r0 = (qm * 4 + qc) * 128
                        if kh == 0:
                            P.inc(g.dma_start(out=t.s_acc[r0:r0 + 128, :],
                                              in_=t.accb[qc % 2][:]),
                                  "ST", 16, ev=("accst", s, qm, qc))
                        else:
                            if t.debug_scratch and s == 0:
                                P.inc(g.dma_start(out=t.s_kh1[r0:r0 + 128, :], in_=t.dbgb[:]),
                                      "ST", 16, ev=("dbgst", s, qm, qc))
                                if qm == 3 and qc == 3:
                                    P.inc(g.dma_start(out=t.s_l[:], in_=t.dbgl[:]), "ST", 16)
                            P.inc(g.dma_start(out=t.y[s * T + r0: s * T + r0 + 128, :],
                                              in_=t.ob[qc % 2][:]),
                                  "ST", 16, ev=("outst", s, qm, qc))
        if on("sync") and s + 1 < n_seq:
            P.wait(g, ("po", s, 1))
            for tt2 in range(2):
                r2 = (s + 1) * T + tt2 * 128
                P.inc(g.dma_start(out=t.xpre[tt2][:], in_=t.x[r2:r2 + 128, :]),
                      "LD", 16, ev=("xload", s + 1, tt2))
        if on("scalar"):
            P.wait(g, ("aev", s, 1, 3, 3))
            P.inc(g.nop(), "SE", 1, ev=("B_done", s))

    if on("sync"):
        P.wait(g, ("outst", n_seq - 1, 3, 3))


_NC_CACHE = {}


def _get_program():
    if "nc" not in _NC_CACHE:
        _NC_CACHE["nc"], _NC_CACHE["t"] = build_program(n_seq=4)
    return _NC_CACHE["nc"], _NC_CACHE["t"]


def kernel(L0_attn_x, L0_attn_Wq, L0_attn_bq, L0_attn_Wk, L0_attn_bk,
           L0_attn_Wv, L0_attn_bv, L0_attn_phase_bias,
           L0_ffn_kernel, L0_ffn_bias, L0_ffn_a, L0_ffn_b):
    x = np.ascontiguousarray(np.asarray(L0_attn_x, dtype=np.float32)).reshape(32 * T, D)
    wqT = np.ascontiguousarray(np.asarray(L0_attn_Wq, dtype=np.float32).T)
    wkT = np.ascontiguousarray(np.asarray(L0_attn_Wk, dtype=np.float32).T)
    wvT = np.ascontiguousarray(np.asarray(L0_attn_Wv, dtype=np.float32).T)
    kern = np.ascontiguousarray(np.asarray(L0_ffn_kernel, dtype=np.float32))
    a_in = np.ascontiguousarray(np.asarray(L0_ffn_a, dtype=np.float32))
    b_in = np.ascontiguousarray(np.asarray(L0_ffn_b, dtype=np.float32))
    bias_in = np.ascontiguousarray(np.asarray(L0_ffn_bias, dtype=np.float32).reshape(1, D))
    bv_in = np.ascontiguousarray(np.asarray(L0_attn_bv, dtype=np.float32).reshape(1, D))
    bq_l = np.ascontiguousarray(np.asarray(L0_attn_bq, dtype=np.float32).reshape(ECH, 128).T)
    bk_l = np.ascontiguousarray(np.asarray(L0_attn_bk, dtype=np.float32).reshape(ECH, 128).T)
    pb_l = np.ascontiguousarray(np.asarray(L0_attn_phase_bias, dtype=np.float32).reshape(ECH, 128).T)
    wl = (np.arange(1, D + 1, dtype=np.float32) * np.float32(2.0 * math.pi / D)).astype(np.float32)
    inv_wl = (np.float32(1.0) / (wl + np.float32(1e-8))).astype(np.float32)
    uscale = (inv_wl * np.float32(1.0 / TWO_PI)).astype(np.float32)
    uscale_in = np.ascontiguousarray(uscale.reshape(ECH, 128).T)

    nc, t = _get_program()
    shared = dict(wqT=wqT, wkT=wkT, wvT=wvT, kern=kern, a_in=a_in, b_in=b_in,
                  bias_in=bias_in, bv_in=bv_in, bq_l=bq_l, bk_l=bk_l, pb_l=pb_l,
                  uscale_in=uscale_in)
    in_maps = []
    for c in range(NCORES):
        m = dict(shared)
        m["x"] = x[c * 4 * T:(c + 1) * 4 * T]
        in_maps.append(m)
    res = run_bass_kernel_spmd(nc, in_maps, core_ids=list(range(NCORES)))
    y = np.concatenate([r["y"] for r in res.results], axis=0).reshape(32, T, D)
    return y


# version 22
# speedup vs baseline: 1.2426x; 1.2426x over previous
import math
import numpy as np
import concourse.bass as bass
import concourse.mybir as mybir
from concourse.bass_utils import run_bass_kernel_spmd

F32 = mybir.dt.float32
I32 = mybir.dt.int32
BF16 = mybir.dt.bfloat16
AOP = mybir.AluOpType
ACT = mybir.ActivationFunctionType
AX = mybir.AxisListType

NCORES = 8
T = 2048
D = 1024
TCH = T // 128
ECH = D // 128
RANK = 16
TWO_PI = float(2.0 * np.pi)
KB = 1024

ENGINES = ["sync", "gpsimd", "vector", "scalar", "tensor"]


class _DummyInst:
    def then_inc(self, *a, **k):
        return self


class _Dummy:
    def __getattr__(self, name):
        return lambda *a, **k: _DummyInst()


class Sched:

    def __init__(self, sems):
        self.sems = sems
        self.counts = {k: 0 for k in sems}
        self.events = {}
        self.mode = "count"

    def inc(self, inst, sem, n=1, ev=None):
        self.counts[sem] += n
        if ev is not None:
            if self.mode == "count":
                assert ev not in self.events, f"dup event {ev}"
                self.events[ev] = (sem, self.counts[sem])
            else:
                assert self.events[ev] == (sem, self.counts[sem]), (ev, self.events[ev], sem, self.counts[sem])
        if self.mode == "emit" and inst is not None and not isinstance(inst, _DummyInst):
            inst.then_inc(self.sems[sem], n)
        return inst

    def wait(self, g, ev):
        if self.mode == "emit":
            assert ev in self.events, f"unknown event {ev}"
            sem, val = self.events[ev]
            g.wait_ge(self.sems[sem], val)

    def last(self, sem):
        return ("__last__", sem, self.counts[sem])

    def wait_last(self, g, sem_val):
        if self.mode == "emit":
            _, sem, val = sem_val
            g.wait_ge(self.sems[sem], val)

    def start_emit(self):
        self.counts = {k: 0 for k in self.counts}
        self.mode = "emit"


class NS:
    pass


def build_program(n_seq=4, debug_scratch=False):
    nc = bass.Bass("TRN2", target_bir_lowering=False)
    t = NS()
    t.n_seq = n_seq

    t.x = nc.dram_tensor("x", [n_seq * T, D], F32, kind="ExternalInput").ap()
    t.y = nc.dram_tensor("y", [n_seq * T, D], F32, kind="ExternalOutput").ap()
    t.wqT = nc.dram_tensor("wqT", [D, D], F32, kind="ExternalInput").ap()
    t.wkT = nc.dram_tensor("wkT", [D, D], F32, kind="ExternalInput").ap()
    t.wvT = nc.dram_tensor("wvT", [D, D], F32, kind="ExternalInput").ap()
    t.kern = nc.dram_tensor("kern", [D, D], F32, kind="ExternalInput").ap()
    t.a_in = nc.dram_tensor("a_in", [D, RANK], F32, kind="ExternalInput").ap()
    t.b_in = nc.dram_tensor("b_in", [RANK, D], F32, kind="ExternalInput").ap()
    t.bias_in = nc.dram_tensor("bias_in", [1, D], F32, kind="ExternalInput").ap()
    t.bv_in = nc.dram_tensor("bv_in", [1, D], F32, kind="ExternalInput").ap()
    t.bq_l = nc.dram_tensor("bq_l", [128, ECH], F32, kind="ExternalInput").ap()
    t.bk_l = nc.dram_tensor("bk_l", [128, ECH], F32, kind="ExternalInput").ap()
    t.pb_l = nc.dram_tensor("pb_l", [128, ECH], F32, kind="ExternalInput").ap()
    t.uscale_in = nc.dram_tensor("uscale_in", [128, ECH], F32, kind="ExternalInput").ap()

    skind = "ExternalOutput" if debug_scratch else "Internal"
    for nm in ["s_cq_hi", "s_cq_lo", "s_sq_hi", "s_sq_lo",
               "s_ck_hi", "s_ck_lo", "s_sk_hi", "s_sk_lo"]:
        setattr(t, nm, nc.dram_tensor(nm, [n_seq, ECH, 128, T], BF16, kind=skind).ap())
    t.s_v_hi = nc.dram_tensor("s_v_hi", [n_seq, TCH, 128, D], BF16, kind=skind).ap()
    t.s_v_lo = nc.dram_tensor("s_v_lo", [n_seq, TCH, 128, D], BF16, kind=skind).ap()
    t.s_acc = nc.dram_tensor("s_acc", [T, D], F32, kind=skind).ap()
    t.debug_scratch = debug_scratch
    if debug_scratch:
        t.s_kh1 = nc.dram_tensor("s_kh1", [T, D], F32, kind="ExternalOutput").ap()
        t.s_l = nc.dram_tensor("s_l", [128, 32], F32, kind="ExternalOutput").ap()
        t.s_part = nc.dram_tensor("s_part", [n_seq * T, D], F32, kind="ExternalOutput").ap()
        t.s_xn = nc.dram_tensor("s_xn", [n_seq * 128, D], F32, kind="ExternalOutput").ap()
        t.s_st = nc.dram_tensor("s_st", [n_seq * 128, 4], F32, kind="ExternalOutput").ap()
    for nm in ["s_wq_hi", "s_wq_lo", "s_wk_hi", "s_wk_lo",
               "s_wv_hi", "s_wv_lo", "s_kq_hi", "s_kq_lo"]:
        setattr(t, nm, nc.dram_tensor(nm, [ECH, 128, D], BF16, kind=skind).ap())

    ARENA = 212736
    arena = nc.alloc_sbuf_tensor("arena", [128, ARENA], mybir.dt.uint8)
    arena_addr = nc.lookup_mloc(arena).addr

    def place(name, shape, dtype, off):
        per_part = int(np.prod(shape[1:])) * mybir.dt.size(dtype)
        assert off + per_part <= ARENA, (name, off, per_part)
        h = nc.virtual_sbuf_tensor(name, list(shape), dtype)
        nc._tensor(h.name, list(shape), dtype, type="SB")
        mloc = nc.lookup_mloc(h)
        mloc.allocated = True
        mloc.addr = arena_addr + off
        return h

    o = [0]

    def pers(name, shape, dtype):
        per_part = int(np.prod(shape[1:])) * mybir.dt.size(dtype)
        h = place(name, shape, dtype, o[0])
        o[0] += (per_part + 31) // 32 * 32
        return h

    t.ident_b = pers("ident_b", [128, 128], BF16)
    t.ident_f = pers("ident_f", [128, 128], F32)
    t.ones_row_b = pers("ones_row_b", [1, 128], BF16)
    t.ones_row_f = pers("ones_row_f", [1, 128], F32)
    t.ones_col_b = pers("ones_col_b", [128, 1], BF16)
    t.c2pi = pers("c2pi", [128, 1], F32)
    t.cinv32 = pers("cinv32", [128, 1], F32)
    t.ceps = pers("ceps", [128, 1], F32)
    t.uscale = pers("uscale", [128, ECH], F32)
    t.bq_sb = pers("bq_sb", [128, ECH], F32)
    t.bk_sb = pers("bk_sb", [128, ECH], F32)
    t.pb_sb = pers("pb_sb", [128, ECH], F32)
    t.ub_qs = pers("ub_qs", [128, ECH], F32)
    t.ub_qc = pers("ub_qc", [128, ECH], F32)
    t.ub_ks = pers("ub_ks", [128, ECH], F32)
    t.ub_kc = pers("ub_kc", [128, ECH], F32)
    t.pb2 = pers("pb2", [128, ECH], F32)
    t.a_f = pers("a_f", [128, ECH * RANK], F32)
    t.a_hi = pers("a_hi", [128, ECH * RANK], BF16)
    t.a_lo = pers("a_lo", [128, ECH * RANK], BF16)
    t.b_hi = pers("b_hi", [RANK, D], BF16)
    t.b_lo = pers("b_lo", [RANK, D], BF16)
    t.bias_hi = pers("bias_hi", [1, D], BF16)
    t.bias_lo = pers("bias_lo", [1, D], BF16)
    t.bv_hi = pers("bv_hi", [1, D], BF16)
    t.bv_lo = pers("bv_lo", [1, D], BF16)
    t.t1_hi = pers("t1_hi", [RANK, T], BF16)
    t.t1_lo = pers("t1_lo", [RANK, T], BF16)
    for nm in ["ssum", "sqs", "mcol", "vcol", "sdcol", "rstd", "am", "msq",
               "gmax", "scale_col", "iscale_col", "ltc", "rcc"]:
        setattr(t, nm, pers(nm, [128, 1], F32))
    t.mxr = pers("mxr", [1, 128], F32)
    t.mx1 = pers("mx1", [1, 1], F32)
    t.vjunk = pers("vjunk", [128, 1], F32)
    t.qth = pers("qth", [128, 14], F32)
    t.sjunk = pers("sjunk", [128, 1], F32)
    t.dbgb = pers("dbgb", [128, D], F32)
    t.dbgl = pers("dbgl", [128, 32], F32)
    t.lsb = pers("lsb", [128, 16], F32)
    assert o[0] <= 29 * KB + 768, o[0]

    AL = (o[0] + 1023) // 1024 * 1024
    assert AL + 178 * KB <= ARENA, AL

    t.xnT_hi = place("xnT_hi", [128, ECH * T], BF16, AL + 0)
    t.xnT_lo = place("xnT_lo", [128, ECH * T], BF16, AL + 32 * KB)
    t.w_hi = place("w_hi", [128, ECH * D], BF16, AL + 64 * KB)
    t.w_lo = place("w_lo", [128, ECH * D], BF16, AL + 80 * KB)
    t.xb = [place("xb0", [128, D], F32, AL + 96 * KB),
            place("xb1", [128, D], F32, AL + 100 * KB)]
    t.po = [place("po0", [128, D], F32, AL + 104 * KB),
            place("po1", [128, D], F32, AL + 108 * KB)]
    t.u_s = place("u_s", [128, T], F32, AL + 112 * KB)
    t.u_c = place("u_c", [128, T], F32, AL + 120 * KB)
    t.n_i = place("n_i", [128, T], I32, AL + 128 * KB)
    t.sin_f = place("sin_f", [128, T], F32, AL + 136 * KB)
    t.u_s2 = place("u_s2", [128, T], F32, AL + 162 * KB)
    t.u_c2 = place("u_c2", [128, T], F32, AL + 170 * KB)
    t.tg = [place(f"tg{i}", [128, T], BF16, AL + 144 * KB + i * 4 * KB) for i in range(4)]
    t.v_f32 = place("v_f32", [128, D], F32, AL + 160 * KB)
    t.v_h = place("v_h", [128, D], BF16, AL + 164 * KB)
    t.v_l = place("v_l", [128, D], BF16, AL + 166 * KB)
    t.b_f = place("b_f", [RANK, D], F32, AL + 160 * KB)
    t.bias_f = place("bias_f", [1, D], F32, AL + 164 * KB)
    t.bv_f = place("bv_f", [1, D], F32, AL + 168 * KB)
    t.xsq = place("xsq", [128, D], F32, AL + 112 * KB)
    t.xn_f = place("xn_f", [128, D], F32, AL + 120 * KB)
    t.wstage = place("wstage", [128, ECH * D], F32, AL + 0)
    t.wstageB = place("wstageB", [128, ECH * D], F32, AL + 32 * KB)
    t.qr1 = place("qr1", [128, ECH * D], F32, AL + 64 * KB)
    t.qr2 = place("qr2", [128, ECH * D], F32, AL + 96 * KB)
    t.wc_hi = place("wc_hi", [128, ECH * D], BF16, AL + 128 * KB)
    t.wc_lo = place("wc_lo", [128, ECH * D], BF16, AL + 144 * KB)
    H = T // 2
    t.cq_hi = place("cq_hi", [128, ECH * 512], BF16, AL + 0)
    t.cq_lo = place("cq_lo", [128, ECH * 512], BF16, AL + 8 * KB)
    t.sq_hi = place("sq_hi", [128, ECH * 512], BF16, AL + 16 * KB)
    t.sq_lo = place("sq_lo", [128, ECH * 512], BF16, AL + 24 * KB)
    t.pt_hi = place("pt_hi", [128, 8 * 512], BF16, AL + 32 * KB)
    t.pt_lo = place("pt_lo", [128, 8 * 512], BF16, AL + 40 * KB)
    t.ob = [place("ob0", [128, D], F32, AL + 48 * KB),
            place("ob1", [128, D], F32, AL + 52 * KB)]
    t.pab = [place("pab0", [128, D], F32, AL + 56 * KB),
             place("pab1", [128, D], F32, AL + 60 * KB)]
    t.vb_hi = place("vb_hi", [128, 8 * D], BF16, AL + 64 * KB)
    t.vb_lo = place("vb_lo", [128, 8 * D], BF16, AL + 80 * KB)
    t.sk_lo = place("sk_lo", [128, ECH * H], BF16, AL + 96 * KB)
    t.ck_hi = place("ck_hi", [128, ECH * H], BF16, AL + 112 * KB)
    t.ck_lo = place("ck_lo", [128, ECH * H], BF16, AL + 128 * KB)
    t.sk_hi = place("sk_hi", [128, ECH * H], BF16, AL + 144 * KB)
    t.p_f = place("p_f", [128, 512], F32, AL + 160 * KB)
    t.accb = [place("accb0", [128, D], F32, AL + 162 * KB),
              place("accb1", [128, D], F32, AL + 166 * KB)]
    t.xpre = [place("xpre0", [128, D], F32, AL + 170 * KB),
              place("xpre1", [128, D], F32, AL + 174 * KB)]

    t.pA = nc.alloc_psum_tensor("pA", [128, 2048], F32)
    t.pB = nc.alloc_psum_tensor("pB", [128, 2048], F32)

    sems = {nm: nc.alloc_semaphore(nm) for nm in ["LD", "ST", "TE", "VE", "SE", "GP"]}
    P = Sched(sems)
    for e in ENGINES:
        _program(t, P, e, _Dummy())
    P.start_emit()
    with nc.Block() as block:
        @block.sync
        def _(g):
            _program(t, P, "sync", g)

        @block.gpsimd
        def _(g):
            _program(t, P, "gpsimd", g)

        @block.vector
        def _(g):
            _program(t, P, "vector", g)

        @block.scalar
        def _(g):
            _program(t, P, "scalar", g)

        @block.tensor
        def _(g):
            _program(t, P, "tensor", g)
    return nc, t


def _mm_group(g, specs):
    seen_first = {}
    lasts = {}
    for i, (out, lhs, rhs, key) in enumerate(specs):
        lasts[key] = i
    inst = None
    for i, (out, lhs, rhs, key) in enumerate(specs):
        st = key not in seen_first
        seen_first[key] = True
        inst = g.matmul(out, lhs, rhs, start=st, stop=(lasts[key] == i),
                        skip_group_check=True)
    return inst




def _emit_po(t, P, g_v, g_s, s, ft, on):
    row0 = s * T + ft * 128
    xbv = t.xpre[ft] if ft < 2 else t.xb[ft % 2]
    pov = t.po[ft % 2]
    if on("vector"):
        P.wait(g_v, ("ffn", s, ft))
        if ft >= 2:
            P.wait(g_v, ("post", s, ft - 2))
        pF = (t.pA if ft % 2 == 0 else t.pB)[:, 1024:2048]
        P.inc(g_v.tensor_tensor(pov[:], xbv[:], pF, AOP.add), "VE", 1, ev=("po", s, ft))
    if on("scalar"):
        P.wait(g_s, ("po", s, ft))
        if t.debug_scratch:
            P.inc(g_s.dma_start(out=t.s_part[row0:row0 + 128, :], in_=pov[:]), "ST", 16)
        P.inc(g_s.dma_start(out=t.y[row0:row0 + 128, :], in_=pov[:]),
              "ST", 16, ev=("post", s, ft))


def _emit_ffn_main(t, P, g, s, ft):
    P.wait(g, ("kqload", s))
    if ft >= 2:
        P.wait(g, ("po", s, ft - 2))
    pF = (t.pA if ft % 2 == 0 else t.pB)[:, 1024:2048]
    specs = []
    for dch in range(ECH):
        for xx, ww in [(t.xnT_hi, t.w_hi), (t.xnT_hi, t.w_lo), (t.xnT_lo, t.w_hi)]:
            for nh in range(2):
                specs.append((pF[:, nh * 512:(nh + 1) * 512],
                              xx[:, dch * T + ft * 128: dch * T + (ft + 1) * 128],
                              ww[:, dch * D + nh * 512: dch * D + (nh + 1) * 512],
                              nh))
    seen = {}
    lastinst = None
    for out, lhs, rhs, key in specs:
        st = key not in seen
        seen[key] = True
        lastinst = g.matmul(out, lhs, rhs, start=st, stop=False, skip_group_check=True)
    P.inc(lastinst, "TE", 1, ev=("ffnm", s, ft))


def _emit_ffn_lora(t, P, g, s, ft):
    P.wait(g, ("t1c", s, ft))
    pF = (t.pA if ft % 2 == 0 else t.pB)[:, 1024:2048]
    specs = []
    for t1x, bb in [(t.t1_hi, t.b_hi), (t.t1_hi, t.b_lo), (t.t1_lo, t.b_hi)]:
        for nh in range(2):
            specs.append((pF[:, nh * 512:(nh + 1) * 512],
                          t1x[:, ft * 128:(ft + 1) * 128],
                          bb[:, nh * 512:(nh + 1) * 512], nh))
    for bb in [t.bias_hi, t.bias_lo]:
        for nh in range(2):
            specs.append((pF[:, nh * 512:(nh + 1) * 512],
                          t.ones_row_b[:], bb[:, nh * 512:(nh + 1) * 512], nh))
    lasts = {}
    for i, (out, lhs, rhs, key) in enumerate(specs):
        lasts[key] = i
    lastinst = None
    for i, (out, lhs, rhs, key) in enumerate(specs):
        lastinst = g.matmul(out, lhs, rhs, start=False, stop=(lasts[key] == i),
                            skip_group_check=True)
    P.inc(lastinst, "TE", 1, ev=("ffn", s, ft))


def _program(t, P, eng, g):
    on = lambda n: eng == n
    n_seq = t.n_seq
    H = T // 2

    if on("gpsimd"):
        g.memset(t.ident_b[:], 0.0)
        g.affine_select(out=t.ident_b[:], in_=t.ident_b[:], compare_op=AOP.not_equal,
                        fill=1.0, base=0, pattern=[[-1, 128]], channel_multiplier=1)
        g.memset(t.ident_f[:], 0.0)
        g.affine_select(out=t.ident_f[:], in_=t.ident_f[:], compare_op=AOP.not_equal,
                        fill=1.0, base=0, pattern=[[-1, 128]], channel_multiplier=1)
        g.memset(t.ones_row_b[:], 1.0)
        g.memset(t.ones_row_f[:], 1.0)
        g.memset(t.ones_col_b[:], 1.0)
        g.memset(t.c2pi[:], TWO_PI)
        g.memset(t.cinv32[:], 1.0 / 32.0)
        P.inc(g.memset(t.ceps[:], 1e-5), "GP", 1, ev="consts")

    if on("sync"):
        for ap_in, ap_sb in [(t.bq_l, t.bq_sb), (t.bk_l, t.bk_sb), (t.pb_l, t.pb_sb),
                             (t.uscale_in, t.uscale), (t.b_in, t.b_f),
                             (t.bias_in, t.bias_f), (t.bv_in, t.bv_f)]:
            P.inc(g.dma_start(out=ap_sb[:], in_=ap_in[:]), "LD", 16)
        P.inc(g.dma_start(out=t.a_f[:].rearrange("p (c r) -> p c r", c=ECH),
                          in_=t.a_in.rearrange("(c p) r -> p c r", p=128)),
              "LD", 16, ev="pl_small")

    if on("vector"):
        P.wait(g, "pl_small")
        P.wait(g, "consts")
        g.tensor_single_scalar(t.pb2[:], t.pb_sb[:], 1.0 / TWO_PI, AOP.mult)
        g.tensor_tensor(t.ub_qs[:], t.bq_sb[:], t.uscale[:], AOP.mult)
        g.tensor_tensor(t.ub_ks[:], t.bk_sb[:], t.uscale[:], AOP.mult)
        g.tensor_tensor(t.ub_qs[:], t.ub_qs[:], t.pb2[:], AOP.add)
        g.tensor_tensor(t.ub_ks[:], t.ub_ks[:], t.pb2[:], AOP.add)
        g.tensor_single_scalar(t.ub_qc[:], t.ub_qs[:], 0.25, AOP.add)
        g.tensor_single_scalar(t.ub_kc[:], t.ub_ks[:], 0.25, AOP.add)
        for f, hi, lo in [(t.a_f, t.a_hi, t.a_lo), (t.b_f, t.b_hi, t.b_lo),
                          (t.bias_f, t.bias_hi, t.bias_lo), (t.bv_f, t.bv_hi, t.bv_lo)]:
            g.tensor_copy(hi[:], f[:])
            P.inc(g.tensor_tensor(lo[:], f[:], hi[:], AOP.subtract), "VE", 1)
        P.inc(g.nop(), "VE", 1, ev="pl_vec")

    wspecs = [("kq", t.kern, t.s_kq_hi, t.s_kq_lo, True),
              ("wq", t.wqT, t.s_wq_hi, t.s_wq_lo, False),
              ("wk", t.wkT, t.s_wk_hi, t.s_wk_lo, False),
              ("wv", t.wvT, t.s_wv_hi, t.s_wv_lo, False)]
    for wi, (nm, src, dst_hi, dst_lo, quant) in enumerate(wspecs):
        wst = t.wstage if wi % 2 == 0 else t.wstageB
        if on("sync"):
            if wi > 1:
                P.wait(g, f"wcast_{wspecs[wi-2][0]}")
            P.inc(g.dma_start(out=wst[:].rearrange("p (c e) -> p c e", c=ECH),
                              in_=src.rearrange("(c p) e -> p c e", p=128)),
                  "LD", 16, ev=f"wload_{nm}")
        if quant:
            if on("vector"):
                P.wait(g, f"wload_{nm}")
                g.tensor_reduce(t.gmax[:], t.wstage[:], axis=AX.X, op=AOP.max,
                                apply_absolute_value=True)
                P.inc(g.tensor_copy(t.vjunk[:], t.c2pi[:]), "VE", 1, ev="q_mx")
            if on("tensor"):
                P.wait(g, "q_mx")
                P.wait(g, "consts")
                P.inc(g.matmul(t.pA[0:1, 0:128], t.gmax[:], t.ident_f[:],
                               start=True, stop=True), "TE", 1, ev="q_tr")
            if on("vector"):
                P.wait(g, "q_tr")
                g.tensor_copy(t.mxr[:], t.pA[0:1, 0:128])
                g.tensor_copy(t.vjunk[:], t.c2pi[:])
                g.tensor_reduce(t.mx1[:], t.mxr[:], axis=AX.X, op=AOP.max,
                                apply_absolute_value=True)
                P.inc(g.tensor_copy(t.vjunk[:], t.c2pi[:]), "VE", 1, ev="q_mx1")
            if on("tensor"):
                P.wait(g, "q_mx1")
                P.inc(g.matmul(t.pA[:, 512:513], t.ones_row_f[:], t.mx1[:],
                               start=True, stop=True), "TE", 1, ev="q_bc")
            if on("vector"):
                P.wait(g, "q_bc")
                g.tensor_copy(t.gmax[:], t.pA[:, 512:513])
                g.tensor_copy(t.vjunk[:], t.c2pi[:])
                g.tensor_single_scalar(t.scale_col[:], t.gmax[:],
                                       float(np.float32(1.0) / np.float32(7.0)), AOP.mult)
                g.tensor_copy(t.vjunk[:], t.c2pi[:])
                for k in range(7):
                    g.tensor_single_scalar(t.qth[:, k:k + 1], t.scale_col[:],
                                           k + 0.5, AOP.mult)
                for k in range(7):
                    g.tensor_single_scalar(t.qth[:, 7 + k:8 + k], t.scale_col[:],
                                           -(k + 0.5), AOP.mult)
                g.tensor_copy(t.vjunk[:], t.c2pi[:])
                g.tensor_single_scalar(t.qr1[:], t.wstage[:], t.qth[:, 0:1], AOP.is_ge)
                g.tensor_single_scalar(t.qr2[:], t.wstage[:], t.qth[:, 7:8], AOP.is_le)
                for k in range(1, 7):
                    g.scalar_tensor_tensor(t.qr1[:], t.wstage[:], t.qth[:, k:k + 1],
                                           t.qr1[:], AOP.is_ge, AOP.add)
                    g.scalar_tensor_tensor(t.qr2[:], t.wstage[:], t.qth[:, 7 + k:8 + k],
                                           t.qr2[:], AOP.is_le, AOP.add)
                g.tensor_tensor(t.qr1[:], t.qr1[:], t.qr2[:], AOP.subtract)
                g.tensor_copy(t.vjunk[:], t.c2pi[:])
                P.inc(g.tensor_single_scalar(t.wstage[:], t.qr1[:], t.scale_col[:],
                                             AOP.mult), "VE", 1, ev="q_done")
        if on("gpsimd"):
            if quant:
                P.wait(g, "q_done")
            else:
                P.wait(g, f"wload_{nm}")
            if wi > 0:
                P.wait(g, f"wspill_{wspecs[wi-1][0]}")
            g.tensor_copy(t.wc_hi[:], wst[:])
            P.inc(g.tensor_tensor(t.wc_lo[:], wst[:], t.wc_hi[:],
                                  AOP.subtract), "GP", 1, ev=f"wcast_{nm}")
        if on("scalar"):
            P.wait(g, f"wcast_{nm}")
            P.inc(g.dma_start(out=dst_hi.rearrange("c p e -> p c e"),
                              in_=t.wc_hi[:].rearrange("p (c e) -> p c e", c=ECH)),
                  "ST", 16)
            P.inc(g.dma_start(out=dst_lo.rearrange("c p e -> p c e"),
                              in_=t.wc_lo[:].rearrange("p (c e) -> p c e", c=ECH)),
                  "ST", 16, ev=f"wspill_{nm}")

    for s in range(n_seq):
        if on("sync"):
            P.wait(g, "wspill_kq")
            if s > 0:
                P.wait(g, ("B_done", s - 1))
            P.inc(g.dma_start(out=t.w_hi[:].rearrange("p (c e) -> p c e", c=ECH),
                              in_=t.s_kq_hi.rearrange("c p e -> p c e")), "LD", 16)
            P.inc(g.dma_start(out=t.w_lo[:].rearrange("p (c e) -> p c e", c=ECH),
                              in_=t.s_kq_lo.rearrange("c p e -> p c e")),
                  "LD", 16, ev=("kqload", s))

        for tt in range(TCH):
            row0 = s * T + tt * 128
            xb = t.xpre[tt] if tt < 2 else t.xb[tt % 2]
            po = t.po[tt % 2]
            if on("sync"):
                if tt < 2:
                    if s == 0:
                        if tt == 0:
                            P.wait(g, "wspill_wv")
                        P.inc(g.dma_start(out=xb[:], in_=t.x[row0:row0 + 128, :]),
                              "LD", 16, ev=("xload", s, tt))
                else:
                    if s == 0 and tt == 2:
                        P.wait(g, "q_done")
                    if tt >= 4:
                        P.wait(g, ("po", s, tt - 2))
                    P.inc(g.dma_start(out=xb[:], in_=t.x[row0:row0 + 128, :]),
                          "LD", 16, ev=("xload", s, tt))
            if on("vector"):
                if s == 0 and tt == 0:
                    P.wait(g, "pl_vec")
                if s > 0 and tt == 0:
                    P.wait(g, ("outst", s - 1, 3, 3))
                P.wait(g, ("xload", s, tt))
                g.tensor_tensor(t.xsq[:], xb[:], xb[:], AOP.mult)
                g.tensor_reduce(t.ssum[:], xb[:], axis=AX.X, op=AOP.add)
                g.tensor_reduce(t.sqs[:], t.xsq[:], axis=AX.X, op=AOP.add)
                g.tensor_single_scalar(t.mcol[:], t.ssum[:], 1.0 / D, AOP.mult)
                g.tensor_single_scalar(t.vcol[:], t.sqs[:], 1.0 / D, AOP.mult)
                g.tensor_tensor(t.msq[:], t.mcol[:], t.mcol[:], AOP.mult)
                g.tensor_copy(t.vjunk[:], t.c2pi[:])
                g.tensor_tensor(t.vcol[:], t.vcol[:], t.msq[:], AOP.subtract)
                g.tensor_copy(t.vjunk[:], t.c2pi[:])
                g.tensor_copy(t.vjunk[:], t.c2pi[:])
                P.inc(g.tensor_copy(t.vjunk[:], t.c2pi[:]), "VE", 1, ev=("var", s, tt))
            if on("scalar"):
                if t.debug_scratch and tt == 1:
                    P.wait(g, ("xn", s, 0))
                    P.inc(g.dma_start(out=t.s_xn[s * 128:(s + 1) * 128, :], in_=t.xn_f[:]),
                          "ST", 16)
                    P.inc(g.dma_start(out=t.s_st[s * 128:(s + 1) * 128, :], in_=t.dbgl[:, 0:4]),
                          "ST", 16, ev=("dbgxn", s))
                P.wait(g, ("var", s, tt))
                if tt == 0:
                    g.activation(t.sjunk[:], t.c2pi[:], ACT.Sqrt)
                    g.activation(t.sjunk[:], t.c2pi[:], ACT.Sqrt)
                g.activation(t.sdcol[:], t.vcol[:], ACT.Sqrt, bias=t.ceps[:])
                g.copy(t.sjunk[:], t.c2pi[:])
                P.inc(g.copy(t.sjunk[:], t.c2pi[:]), "SE", 1, ev=("sd", s, tt))
            if on("vector"):
                P.wait(g, ("sd", s, tt))
                if t.debug_scratch and tt == 1:
                    P.wait(g, ("dbgxn", s))
                if tt > 0:
                    P.wait(g, ("trl", s, tt - 1))
                g.reciprocal(t.rstd[:], t.sdcol[:])
                g.tensor_copy(t.vjunk[:], t.c2pi[:])
                g.tensor_tensor(t.am[:], t.mcol[:], t.rstd[:], AOP.mult)
                g.tensor_copy(t.vjunk[:], t.c2pi[:])
                g.tensor_copy(t.vjunk[:], t.c2pi[:])
                if t.debug_scratch and tt == 0:
                    g.tensor_copy(t.dbgl[:, 0:1], t.rstd[:])
                    g.tensor_copy(t.dbgl[:, 1:2], t.am[:])
                    g.tensor_copy(t.dbgl[:, 2:3], t.mcol[:])
                    g.tensor_copy(t.dbgl[:, 3:4], t.vcol[:])
                P.inc(g.tensor_scalar(t.xn_f[:], xb[:], t.rstd[:], t.am[:],
                                      AOP.mult, AOP.subtract), "VE", 1, ev=("xn", s, tt))
            if on("tensor"):
                if tt >= 1:
                    _emit_ffn_main(t, P, g, s, tt - 1)
                P.wait(g, ("xn", s, tt))
                if tt > 0:
                    P.wait(g, ("xnT", s, tt - 1))
                if s == 0 and tt == 0:
                    P.wait(g, "consts")
                for dch in range(ECH):
                    inst = g.transpose(t.pA[:, dch * 128:(dch + 1) * 128],
                                       t.xn_f[:, dch * 128:(dch + 1) * 128], t.ident_f[:])
                P.inc(inst, "TE", 1, ev=("trl", s, tt))
            if on("vector"):
                P.wait(g, ("trl", s, tt))
                if s == 0 and tt == 0:
                    P.wait(g, "wcast_wv")
                xnT_hi_v = t.xnT_hi[:].rearrange("p (c n) -> p c n", c=ECH)
                xnT_lo_v = t.xnT_lo[:].rearrange("p (c n) -> p c n", c=ECH)
                tr_v = t.pA[:, 0:1024].rearrange("p (c n) -> p c n", c=ECH)
                g.tensor_copy(xnT_hi_v[:, :, tt * 128:(tt + 1) * 128], tr_v)
                P.inc(g.tensor_tensor(xnT_lo_v[:, :, tt * 128:(tt + 1) * 128], tr_v,
                                      xnT_hi_v[:, :, tt * 128:(tt + 1) * 128],
                                      AOP.subtract),
                      "VE", 1, ev=("xnT", s, tt))
            if on("tensor"):
                P.wait(g, ("xnT", s, tt))
                if s == 0 and tt == 0:
                    P.wait(g, "pl_vec")
                if tt > 0:
                    P.wait(g, ("t1c", s, tt - 1))
                t1ps = t.pB[0:RANK, 0:128]
                specs = []
                for dch in range(ECH):
                    for xx, aa in [(t.xnT_hi, t.a_hi), (t.xnT_hi, t.a_lo), (t.xnT_lo, t.a_hi)]:
                        specs.append((t1ps,
                                      aa[:, dch * RANK:(dch + 1) * RANK],
                                      xx[:, dch * T + tt * 128: dch * T + (tt + 1) * 128],
                                      0))
                P.inc(_mm_group(g, specs), "TE", 1, ev=("t1ps", s, tt))
                if tt >= 1:
                    _emit_ffn_lora(t, P, g, s, tt - 1)
            if on("vector"):
                P.wait(g, ("t1ps", s, tt))
                t1ps = t.pB[0:RANK, 0:128]
                g.tensor_copy(t.t1_hi[:, tt * 128:(tt + 1) * 128], t1ps)
                P.inc(g.tensor_tensor(t.t1_lo[:, tt * 128:(tt + 1) * 128], t1ps,
                                      t.t1_hi[:, tt * 128:(tt + 1) * 128], AOP.subtract),
                      "VE", 1, ev=("t1c", s, tt))
            if tt >= 1:
                _emit_po(t, P, g, g, s, tt - 1, on)


        if on("tensor"):
            _emit_ffn_main(t, P, g, s, TCH - 1)
            _emit_ffn_lora(t, P, g, s, TCH - 1)
        _emit_po(t, P, g, g, s, TCH - 1, on)

        for which, srcs, spill_sin, spill_cos, ub_s, ub_c in [
                ("q", (t.s_wq_hi, t.s_wq_lo), (t.s_sq_hi, t.s_sq_lo),
                 (t.s_cq_hi, t.s_cq_lo), t.ub_qs, t.ub_qc),
                ("k", (t.s_wk_hi, t.s_wk_lo), (t.s_sk_hi, t.s_sk_lo),
                 (t.s_ck_hi, t.s_ck_lo), t.ub_ks, t.ub_kc)]:
            if on("sync"):
                P.wait(g, f"wspill_w{which}")
                if which == "q":
                    P.wait(g, ("ffn", s, TCH - 1))
                else:
                    P.wait(g, ("qk", s, "q", ECH - 1))
                P.inc(g.dma_start(out=t.w_hi[:].rearrange("p (c e) -> p c e", c=ECH),
                                  in_=srcs[0].rearrange("c p e -> p c e")), "LD", 16)
                P.inc(g.dma_start(out=t.w_lo[:].rearrange("p (c e) -> p c e", c=ECH),
                                  in_=srcs[1].rearrange("c p e -> p c e")),
                      "LD", 16, ev=("wld", s, which))
            for ec in range(ECH):
                pp = t.pA if ec % 2 == 0 else t.pB
                if on("tensor"):
                    P.wait(g, ("wld", s, which))
                    if which == "q" and ec < 2:
                        P.wait(g, ("po", s, TCH - 1))
                    prev = None
                    if ec >= 2:
                        prev = (s, which, ec - 2)
                    elif which == "k":
                        prev = (s, "q", ec + ECH - 2)
                    if prev is not None:
                        P.wait(g, ("uev", *prev))
                    specs = []
                    for dch in range(ECH):
                        for xx, ww in [(t.xnT_hi, t.w_hi), (t.xnT_hi, t.w_lo), (t.xnT_lo, t.w_hi)]:
                            for nq in range(4):
                                specs.append((pp[:, nq * 512:(nq + 1) * 512],
                                              ww[:, dch * D + ec * 128: dch * D + (ec + 1) * 128],
                                              xx[:, dch * T + nq * 512: dch * T + (nq + 1) * 512],
                                              nq))
                    P.inc(_mm_group(g, specs), "TE", 1, ev=("qk", s, which, ec))
                gp_ = (0 if which == "q" else ECH) + ec
                us_c = t.u_s if gp_ % 2 == 0 else t.u_s2
                uc_c = t.u_c if gp_ % 2 == 0 else t.u_c2
                us_n = t.u_s2 if gp_ % 2 == 0 else t.u_s
                uc_n = t.u_c2 if gp_ % 2 == 0 else t.u_c
                has_next = not (which == "k" and ec == ECH - 1)

                def _emit_a_next():
                    nwhich, nec = (which, ec + 1) if ec + 1 < ECH else ("k", 0)
                    nub_s = t.ub_qs if nwhich == "q" else t.ub_ks
                    nub_c = t.ub_qc if nwhich == "q" else t.ub_kc
                    ppn = t.pA if nec % 2 == 0 else t.pB
                    P.wait(g, ("qk", s, nwhich, nec))
                    g.tensor_scalar(us_n[:], ppn[:], t.uscale[:, nec:nec + 1],
                                    nub_s[:, nec:nec + 1], AOP.mult, AOP.add)
                    P.inc(g.tensor_scalar(uc_n[:], ppn[:], t.uscale[:, nec:nec + 1],
                                          nub_c[:, nec:nec + 1], AOP.mult, AOP.add),
                          "VE", 1, ev=("uev", s, nwhich, nec))

                if on("vector"):
                    if which == "q" and ec == 0:
                        P.wait(g, ("qk", s, "q", 0))
                        g.tensor_scalar(us_c[:], pp[:], t.uscale[:, 0:1],
                                        ub_s[:, 0:1], AOP.mult, AOP.add)
                        P.inc(g.tensor_scalar(uc_c[:], pp[:], t.uscale[:, 0:1],
                                              ub_c[:, 0:1], AOP.mult, AOP.add),
                              "VE", 1, ev=("uev", s, "q", 0))
                    g.tensor_copy(t.n_i[:], us_c[:])
                    P.inc(g.tensor_tensor(t.sin_f[:], us_c[:], t.n_i[:], AOP.subtract),
                          "VE", 1, ev=("fs", s, which, ec))
                    if has_next and ec < ECH - 1:
                        _emit_a_next()
                    P.wait(g, ("sinact", s, which, ec))
                    g.tensor_copy(t.n_i[:], uc_c[:])
                    P.inc(g.tensor_tensor(t.sin_f[:], uc_c[:], t.n_i[:], AOP.subtract),
                          "VE", 1, ev=("fc", s, which, ec))
                    if ec > 0:
                        P.wait(g, ("spill", s, which, ec - 1))
                    elif which == "k":
                        P.wait(g, ("spill", s, "q", ECH - 1))
                    g.tensor_copy(t.tg[0][:], us_c[:])
                    g.tensor_tensor(t.tg[1][:], us_c[:], t.tg[0][:], AOP.subtract)
                    P.wait(g, ("cosact", s, which, ec))
                    g.tensor_copy(t.tg[2][:], uc_c[:])
                    P.inc(g.tensor_tensor(t.tg[3][:], uc_c[:], t.tg[2][:], AOP.subtract),
                          "VE", 1, ev=("trig", s, which, ec))
                    if has_next and ec == ECH - 1:
                        _emit_a_next()
                if on("scalar"):
                    P.wait(g, ("fs", s, which, ec))
                    if which == "q" and ec == 0:
                        g.activation(t.sjunk[:], t.c2pi[:], ACT.Sin)
                        g.activation(t.sjunk[:], t.c2pi[:], ACT.Sin)
                    P.inc(g.activation(us_c[:], t.sin_f[:], ACT.Sin, scale=t.c2pi[:]),
                          "SE", 1, ev=("sinact", s, which, ec))
                if on("scalar"):
                    P.wait(g, ("fc", s, which, ec))
                    P.inc(g.activation(uc_c[:], t.sin_f[:], ACT.Sin, scale=t.c2pi[:]),
                          "SE", 1, ev=("cosact", s, which, ec))
                if on("scalar"):
                    P.wait(g, ("trig", s, which, ec))
                    P.inc(g.dma_start(out=spill_sin[0][s, ec], in_=t.tg[0][:]), "ST", 16)
                    P.inc(g.dma_start(out=spill_sin[1][s, ec], in_=t.tg[1][:]), "ST", 16)
                    P.inc(g.dma_start(out=spill_cos[0][s, ec], in_=t.tg[2][:]), "ST", 16)
                    P.inc(g.dma_start(out=spill_cos[1][s, ec], in_=t.tg[3][:]),
                          "ST", 16, ev=("spill", s, which, ec))

        if on("sync"):
            P.wait(g, "wspill_wv")
            P.wait(g, ("qk", s, "k", ECH - 1))
            P.inc(g.dma_start(out=t.w_hi[:].rearrange("p (c e) -> p c e", c=ECH),
                              in_=t.s_wv_hi.rearrange("c p e -> p c e")), "LD", 16)
            P.inc(g.dma_start(out=t.w_lo[:].rearrange("p (c e) -> p c e", c=ECH),
                              in_=t.s_wv_lo.rearrange("c p e -> p c e")),
                  "LD", 16, ev=("wld", s, "v"))
        if on("sync"):
            P.wait(g, ("spill", s, "k", ECH - 1))
            P.wait(g, ("trig", s, "k", ECH - 1))
            P.wait(g, ("cosact", s, "k", ECH - 1))
            for di, (dst, ssrc) in enumerate([(t.ck_hi, t.s_ck_hi), (t.ck_lo, t.s_ck_lo),
                                              (t.sk_hi, t.s_sk_hi), (t.sk_lo, t.s_sk_lo)]):
                P.inc(g.dma_start(out=dst[:].rearrange("p (c n) -> p c n", c=ECH),
                                  in_=ssrc[s][:, :, 0:H].rearrange("c p n -> p c n")),
                      "LD", 16, ev=("cks", s, 0) if di == 3 else None)
        for kc in range(TCH):
            pv = (t.pA if kc % 2 == 0 else t.pB)[:, 0:1024]
            if on("tensor"):
                P.wait(g, ("wld", s, "v"))
                if kc < 2:
                    P.wait(g, ("uev", s, "k", ECH - 2 + kc))
                else:
                    P.wait(g, ("vev", s, kc - 2))
                specs = []
                for dch in range(ECH):
                    for xx, ww in [(t.xnT_hi, t.w_hi), (t.xnT_hi, t.w_lo), (t.xnT_lo, t.w_hi)]:
                        for nh in range(2):
                            specs.append((pv[:, nh * 512:(nh + 1) * 512],
                                          xx[:, dch * T + kc * 128: dch * T + (kc + 1) * 128],
                                          ww[:, dch * D + nh * 512: dch * D + (nh + 1) * 512],
                                          nh))
                for bb in [t.bv_hi, t.bv_lo]:
                    for nh in range(2):
                        specs.append((pv[:, nh * 512:(nh + 1) * 512],
                                      t.ones_row_b[:], bb[:, nh * 512:(nh + 1) * 512], nh))
                P.inc(_mm_group(g, specs), "TE", 1, ev=("vps", s, kc))
            if on("vector"):
                P.wait(g, ("vps", s, kc))
                if kc == 0:
                    P.wait(g, ("spill", s, "k", ECH - 1))
                else:
                    P.wait(g, ("vspill", s, kc - 1))
                g.tensor_copy(t.v_f32[:], pv)
                g.tensor_copy(t.v_h[:], t.v_f32[:])
                P.inc(g.tensor_tensor(t.v_l[:], t.v_f32[:], t.v_h[:], AOP.subtract),
                      "VE", 1, ev=("vev", s, kc))
            if on("scalar"):
                P.wait(g, ("vev", s, kc))
                P.inc(g.dma_start(out=t.s_v_hi[s, kc], in_=t.v_h[:]), "ST", 16)
                P.inc(g.dma_start(out=t.s_v_lo[s, kc], in_=t.v_l[:]),
                      "ST", 16, ev=("vspill", s, kc))

        for kh in range(2):
            if on("sync"):
                if kh == 0:
                    P.wait(g, ("vps", s, TCH - 1))
                else:
                    P.wait(g, ("sc", s, 0, 3, 7))
                    for di, (dst, ssrc) in enumerate([(t.ck_hi, t.s_ck_hi), (t.ck_lo, t.s_ck_lo),
                                                      (t.sk_hi, t.s_sk_hi), (t.sk_lo, t.s_sk_lo)]):
                        P.inc(g.dma_start(out=dst[:].rearrange("p (c n) -> p c n", c=ECH),
                                          in_=ssrc[s][:, :, H:T].rearrange("c p n -> p c n")),
                              "LD", 16, ev=("cks", s, 1) if di == 3 else None)
                for di, (dst, ssrc) in enumerate([(t.cq_hi, t.s_cq_hi), (t.cq_lo, t.s_cq_lo),
                                                  (t.sq_hi, t.s_sq_hi), (t.sq_lo, t.s_sq_lo)]):
                    P.inc(g.dma_start(out=dst[:].rearrange("p (c n) -> p c n", c=ECH),
                                      in_=ssrc[s][:, :, 0:512].rearrange("c p n -> p c n")),
                          "LD", 16, ev=("cq", s, kh, 0) if di == 3 else None)
                if kh == 0:
                    P.wait(g, ("vspill", s, TCH - 1))
                    P.wait(g, ("vev", s, TCH - 1))
                else:
                    P.wait(g, ("attn", s, 0, 3, 3))
                for di, (dst, ssrc) in enumerate([(t.vb_hi, t.s_v_hi), (t.vb_lo, t.s_v_lo)]):
                    P.inc(g.dma_start(out=dst[:].rearrange("p (c n) -> p c n", c=8),
                                      in_=ssrc[s][kh * 8:(kh + 1) * 8].rearrange("c p n -> p c n")),
                          "LD", 16, ev=("ckv", s, kh) if di == 1 else None)
            for qm in range(4):
                if on("sync"):
                    if qm > 0:
                        P.wait(g, ("sc", s, kh, qm - 1, 7))
                        for di, (dst, ssrc) in enumerate([(t.cq_hi, t.s_cq_hi), (t.cq_lo, t.s_cq_lo),
                                                          (t.sq_hi, t.s_sq_hi), (t.sq_lo, t.s_sq_lo)]):
                            P.inc(g.dma_start(out=dst[:].rearrange("p (c n) -> p c n", c=ECH),
                                              in_=ssrc[s][:, :, qm * 512:(qm + 1) * 512].rearrange("c p n -> p c n")),
                                  "LD", 16, ev=("cq", s, kh, qm) if di == 3 else None)
                    if kh == 1:
                        for qc in range(4):
                            r0 = (qm * 4 + qc) * 128
                            P.wait(g, ("accst", s, 3, 3))
                            if not (qm == 0 and qc < 2):
                                pq = (qm, qc - 2) if qc >= 2 else (qm - 1, qc + 2)
                                P.wait(g, ("aev", s, 1, *pq))
                            P.inc(g.dma_start(out=t.accb[qc % 2][:],
                                              in_=t.s_acc[r0:r0 + 128, :]),
                                  "LD", 16, ev=("accld", s, qm, qc))
                            P.inc(g.dma_start(out=t.pab[qc % 2][:],
                                              in_=t.y[s * T + r0: s * T + r0 + 128, :]),
                                  "LD", 16, ev=("pald", s, qm, qc))
                for kc in range(8):
                    ps = t.pA[:, (kc % 2) * 512:(kc % 2) * 512 + 512]
                    if on("tensor"):
                        if qm == 0 and kc == 0:
                            P.wait(g, ("cks", s, kh))
                        P.wait(g, ("cq", s, kh, qm))
                        if kc >= 2:
                            P.wait(g, ("exp", s, kh, qm, kc - 2))
                        elif qm > 0 or kh > 0:
                            pqm = (s, kh, qm - 1) if qm > 0 else (s, kh - 1, 3)
                            P.wait(g, ("exp", *pqm, 6 + kc))
                        specs = []
                        for ec in range(ECH):
                            for ckx, cqx in [(t.ck_hi, t.cq_hi), (t.ck_lo, t.cq_hi),
                                             (t.ck_hi, t.cq_lo), (t.sk_hi, t.sq_hi),
                                             (t.sk_lo, t.sq_hi), (t.sk_hi, t.sq_lo)]:
                                specs.append((ps,
                                              ckx[:, ec * H + kc * 128: ec * H + (kc + 1) * 128],
                                              cqx[:, ec * 512:(ec + 1) * 512], 0))
                        P.inc(_mm_group(g, specs), "TE", 1, ev=("sc", s, kh, qm, kc))
                    if on("scalar"):
                        P.wait(g, ("sc", s, kh, qm, kc))
                        if kc >= 1:
                            P.wait(g, ("pt", s, kh, qm, kc - 1))
                        if kh == 0 and qm == 0 and kc == 0:
                            g.activation(t.sjunk[:], t.c2pi[:], ACT.Exp)
                            g.activation(t.sjunk[:], t.c2pi[:], ACT.Exp)
                        P.inc(g.activation(t.p_f[:], ps, ACT.Exp, scale=t.cinv32[:]),
                              "SE", 1, ev=("exp", s, kh, qm, kc))
                    if on("vector"):
                        P.wait(g, ("exp", s, kh, qm, kc))
                        g.tensor_copy(t.pt_hi[:, kc * 512:(kc + 1) * 512], t.p_f[:])
                        P.inc(g.tensor_tensor(t.pt_lo[:, kc * 512:(kc + 1) * 512], t.p_f[:],
                                              t.pt_hi[:, kc * 512:(kc + 1) * 512],
                                              AOP.subtract),
                              "VE", 1, ev=("pt", s, kh, qm, kc))
                for qc in range(4):
                    pa_ = t.pB[:, (qc % 2) * 1024:(qc % 2) * 1024 + 1024]
                    lcol = t.pA[:, 1024 + (qc % 2) * 512 + qc: 1024 + (qc % 2) * 512 + qc + 1]
                    if on("tensor"):
                        P.wait(g, ("pt", s, kh, qm, 7))
                        if qm == 0 and qc == 0:
                            P.wait(g, ("ckv", s, kh))
                        if qc >= 2:
                            P.wait(g, ("aev", s, kh, qm, qc - 2))
                        elif qm > 0 or kh > 0:
                            pqm = (s, kh, qm - 1) if qm > 0 else (s, kh - 1, 3)
                            P.wait(g, ("aev", *pqm, qc + 2))
                        if qm > 0 or kh > 0:
                            pqm = (s, kh, qm - 1) if qm > 0 else (s, kh - 1, 3)
                            P.wait(g, ("lread", *pqm, qc))
                        specs = []
                        for kc in range(8):
                            for px, vx, dol in [(t.pt_hi, t.vb_hi, True),
                                                (t.pt_hi, t.vb_lo, False),
                                                (t.pt_lo, t.vb_hi, True)]:
                                lhs = px[:, kc * 512 + qc * 128: kc * 512 + (qc + 1) * 128]
                                for nh in range(2):
                                    specs.append((pa_[:, nh * 512:(nh + 1) * 512], lhs,
                                                  vx[:, kc * D + nh * 512: kc * D + (nh + 1) * 512],
                                                  nh))
                                if dol:
                                    specs.append((lcol, lhs, t.ones_col_b[:], 2))
                        P.inc(_mm_group(g, specs), "TE", 1, ev=("attn", s, kh, qm, qc))
                    if on("vector"):
                        P.wait(g, ("attn", s, kh, qm, qc))
                        if kh == 0:
                            ab = t.accb[qc % 2]
                            if not (qm == 0 and qc < 2):
                                pq = (qm, qc - 2) if qc >= 2 else (qm - 1, qc + 2)
                                P.wait(g, ("accst", s, *pq))
                            g.tensor_copy(ab[:], pa_)
                            P.inc(g.tensor_copy(t.lsb[:, qm * 4 + qc: qm * 4 + qc + 1],
                                                lcol), "VE", 1, ev=("lread", s, kh, qm, qc))
                            P.inc(g.nop(), "VE", 1, ev=("aev", s, kh, qm, qc))
                        else:
                            ob = t.ob[qc % 2]
                            if t.debug_scratch and s == 0:
                                if not (qm == 0 and qc == 0):
                                    pq2 = (qm, qc - 1) if qc >= 1 else (qm - 1, 3)
                                    P.wait(g, ("dbgst", s, *pq2))
                                g.tensor_copy(t.dbgb[:], pa_)
                                g.tensor_copy(t.dbgl[:, qm * 8 + qc: qm * 8 + qc + 1], lcol)
                                g.tensor_copy(t.dbgl[:, qm * 8 + 4 + qc: qm * 8 + 4 + qc + 1],
                                              t.lsb[:, qm * 4 + qc: qm * 4 + qc + 1])
                            g.tensor_tensor(t.ltc[:], lcol,
                                            t.lsb[:, qm * 4 + qc: qm * 4 + qc + 1], AOP.add)
                            g.tensor_copy(t.vjunk[:], t.c2pi[:])
                            g.reciprocal(t.rcc[:], t.ltc[:])
                            P.inc(g.tensor_copy(t.vjunk[:], t.c2pi[:]), "VE", 1,
                                  ev=("lread", s, kh, qm, qc))
                            P.wait(g, ("accld", s, qm, qc))
                            P.wait(g, ("pald", s, qm, qc))
                            if not (qm == 0 and qc < 2):
                                pq = (qm, qc - 2) if qc >= 2 else (qm - 1, qc + 2)
                                P.wait(g, ("outst", s, *pq))
                            elif s > 0:
                                P.wait(g, ("outst", s - 1, 3, qc + 2))
                            g.tensor_tensor(ob[:], t.accb[qc % 2][:], pa_, AOP.add)
                            g.tensor_single_scalar(ob[:], ob[:], t.rcc[:], AOP.mult)
                            P.inc(g.tensor_tensor(ob[:], ob[:], t.pab[qc % 2][:], AOP.add),
                                  "VE", 1, ev=("aev", s, kh, qm, qc))
                    if on("scalar"):
                        P.wait(g, ("aev", s, kh, qm, qc))
                        r0 = (qm * 4 + qc) * 128
                        if kh == 0:
                            P.inc(g.dma_start(out=t.s_acc[r0:r0 + 128, :],
                                              in_=t.accb[qc % 2][:]),
                                  "ST", 16, ev=("accst", s, qm, qc))
                        else:
                            if t.debug_scratch and s == 0:
                                P.inc(g.dma_start(out=t.s_kh1[r0:r0 + 128, :], in_=t.dbgb[:]),
                                      "ST", 16, ev=("dbgst", s, qm, qc))
                                if qm == 3 and qc == 3:
                                    P.inc(g.dma_start(out=t.s_l[:], in_=t.dbgl[:]), "ST", 16)
                            P.inc(g.dma_start(out=t.y[s * T + r0: s * T + r0 + 128, :],
                                              in_=t.ob[qc % 2][:]),
                                  "ST", 16, ev=("outst", s, qm, qc))
        if on("sync") and s + 1 < n_seq:
            P.wait(g, ("po", s, 1))
            for tt2 in range(2):
                r2 = (s + 1) * T + tt2 * 128
                P.inc(g.dma_start(out=t.xpre[tt2][:], in_=t.x[r2:r2 + 128, :]),
                      "LD", 16, ev=("xload", s + 1, tt2))
        if on("scalar"):
            P.wait(g, ("aev", s, 1, 3, 3))
            P.inc(g.nop(), "SE", 1, ev=("B_done", s))

    if on("sync"):
        P.wait(g, ("outst", n_seq - 1, 3, 3))


_NC_CACHE = {}


def _get_program():
    if "nc" not in _NC_CACHE:
        _NC_CACHE["nc"], _NC_CACHE["t"] = build_program(n_seq=4)
    return _NC_CACHE["nc"], _NC_CACHE["t"]


def kernel(L0_attn_x, L0_attn_Wq, L0_attn_bq, L0_attn_Wk, L0_attn_bk,
           L0_attn_Wv, L0_attn_bv, L0_attn_phase_bias,
           L0_ffn_kernel, L0_ffn_bias, L0_ffn_a, L0_ffn_b):
    x = np.ascontiguousarray(np.asarray(L0_attn_x, dtype=np.float32)).reshape(32 * T, D)
    wqT = np.ascontiguousarray(np.asarray(L0_attn_Wq, dtype=np.float32).T)
    wkT = np.ascontiguousarray(np.asarray(L0_attn_Wk, dtype=np.float32).T)
    wvT = np.ascontiguousarray(np.asarray(L0_attn_Wv, dtype=np.float32).T)
    kern = np.ascontiguousarray(np.asarray(L0_ffn_kernel, dtype=np.float32))
    a_in = np.ascontiguousarray(np.asarray(L0_ffn_a, dtype=np.float32))
    b_in = np.ascontiguousarray(np.asarray(L0_ffn_b, dtype=np.float32))
    bias_in = np.ascontiguousarray(np.asarray(L0_ffn_bias, dtype=np.float32).reshape(1, D))
    bv_in = np.ascontiguousarray(np.asarray(L0_attn_bv, dtype=np.float32).reshape(1, D))
    bq_l = np.ascontiguousarray(np.asarray(L0_attn_bq, dtype=np.float32).reshape(ECH, 128).T)
    bk_l = np.ascontiguousarray(np.asarray(L0_attn_bk, dtype=np.float32).reshape(ECH, 128).T)
    pb_l = np.ascontiguousarray(np.asarray(L0_attn_phase_bias, dtype=np.float32).reshape(ECH, 128).T)
    wl = (np.arange(1, D + 1, dtype=np.float32) * np.float32(2.0 * math.pi / D)).astype(np.float32)
    inv_wl = (np.float32(1.0) / (wl + np.float32(1e-8))).astype(np.float32)
    uscale = (inv_wl * np.float32(1.0 / TWO_PI)).astype(np.float32)
    uscale_in = np.ascontiguousarray(uscale.reshape(ECH, 128).T)

    nc, t = _get_program()
    shared = dict(wqT=wqT, wkT=wkT, wvT=wvT, kern=kern, a_in=a_in, b_in=b_in,
                  bias_in=bias_in, bv_in=bv_in, bq_l=bq_l, bk_l=bk_l, pb_l=pb_l,
                  uscale_in=uscale_in)
    in_maps = []
    for c in range(NCORES):
        m = dict(shared)
        m["x"] = x[c * 4 * T:(c + 1) * 4 * T]
        in_maps.append(m)
    res = run_bass_kernel_spmd(nc, in_maps, core_ids=list(range(NCORES)))
    y = np.concatenate([r["y"] for r in res.results], axis=0).reshape(32, T, D)
    return y
